# version 05
# baseline (speedup 1.0000x reference)
import math
from contextlib import ExitStack

import numpy as np
import concourse.bass as bass
import concourse.mybir as mybir
from concourse.bass_utils import run_bass_kernel_spmd

F32 = mybir.dt.float32
BF16 = mybir.dt.bfloat16
U32 = mybir.dt.uint32
U8 = mybir.dt.uint8
AF = mybir.ActivationFunctionType
ALU = mybir.AluOpType
AX = mybir.AxisListType

B8 = 8
T = 2048
D = 4096
J = 22016
FF = 11008
NTT = 16
NTC = 4
NKT = 32
NA = 86
NCH = 32
EPS = 1e-5
INVS = 1.0 / math.sqrt(2048.0)


class Ctr:
    def __init__(self, ctx, nc, name, step=1):
        self.sem = ctx.enter_context(nc.semaphore(name))
        self.step = step
        self.v = 0

    def inc(self, inst):
        self.v += self.step
        inst.then_inc(self.sem, self.step)
        return self.v


def build():
    nc = bass.Bass("TRN2", target_bir_lowering=False)

    x = nc.dram_tensor("x", [B8, T, D], F32, kind="ExternalInput")
    x_tok = nc.dram_tensor("x_tok", [T, D], F32, kind="ExternalInput")
    qprep = nc.dram_tensor("qprep", [NA, 2, 128, 2048], U8,
                           kind="ExternalInput")
    srprep = nc.dram_tensor("srprep", [NA, 2, 128, 4, 128], F32,
                            kind="ExternalInput")
    sc_gu = nc.dram_tensor("sc_gu", [32, J], F32, kind="ExternalInput")
    qz_gu = nc.dram_tensor("qz_gu", [32, J // 8], U32, kind="ExternalInput")
    qdprep = nc.dram_tensor("qdprep", [11, 128, 4, D], U8,
                            kind="ExternalInput")
    srep_d = nc.dram_tensor("srep_d", [1376, D], F32, kind="ExternalInput")
    sc_d = nc.dram_tensor("sc_d", [86, D], F32, kind="ExternalInput")
    qz_d = nc.dram_tensor("qz_d", [86, D // 8], U32, kind="ExternalInput")
    out = nc.dram_tensor("out", [T, D], F32, kind="ExternalOutput")

    h_dram = nc.dram_tensor("h_dram", [FF, T], BF16, kind="Internal")
    msg_dram = nc.dram_tensor("msg_dram", [T, D], BF16, kind="Internal")
    bgu_dram = nc.dram_tensor("bgu_dram", [32, J], BF16, kind="Internal")
    bd_dram = nc.dram_tensor("bd_dram", [86, D], BF16, kind="Internal")

    xap = x.ap()
    xtk = x_tok.ap()
    out_ap = out.ap()
    bgu_ap = bgu_dram.ap()
    bd_ap = bd_dram.ap()
    msg_ap = msg_dram.ap()

    with ExitStack() as top:
        SQ = Ctr(top, nc, "SQ", 16)
        AQ = Ctr(top, nc, "AQ", 16)
        V = Ctr(top, nc, "V")
        S = Ctr(top, nc, "S")
        TE = Ctr(top, nc, "TE")
        G = Ctr(top, nc, "G")
        P0L = [Ctr(top, nc, "P0L%d" % i, 16) for i in range(4)]
        P2L = [Ctr(top, nc, "P2L%d" % i, 16) for i in range(2)]
        P3L = [Ctr(top, nc, "P3L%d" % i, 16) for i in range(2)]
        SP3 = [Ctr(top, nc, "SP3%d" % i, 16) for i in range(2)]
        P4W = [Ctr(top, nc, "P4W%d" % i, 16) for i in range(2)]
        BDC = Ctr(top, nc, "BDC", 16)
        P4H = [Ctr(top, nc, "P4H%d" % i, 16) for i in range(2)]
        ST4 = [Ctr(top, nc, "ST4%d" % i, 16) for i in range(2)]
        SCD = Ctr(top, nc, "SCD", 16)
        allq = ([SQ, AQ, V, S, TE, G] + P0L + P2L + P3L + SP3 + P4W +
                [BDC] + P4H + ST4 + [SCD])

        sb = lambda *a: top.enter_context(nc.sbuf_tensor(*a))
        ident = sb("ident", [128, 128], BF16)
        scratch_f = sb("scratch_f", [128, 172], F32)
        indneg = sb("indneg", [128, 172], BF16)
        ones1 = sb("ones1", [1, 128], BF16)
        stats_b = sb("stats_b", [128, NTT, 3], F32)
        stats_all = sb("stats_all", [128, B8, NTT, 3], F32)
        nrs_t = sb("nrs_t", [32, T], BF16)
        nrsh_t = sb("nrsh_t", [86, T], BF16)
        S_all = sb("S_all", [128, NCH, 8], F32)
        E_all = sb("E_all", [128, NCH, 8], BF16)

        block = top.enter_context(nc.Block())

        def barrier():
            snap = [(q.sem, q.v) for q in allq if q.v]
            for opener in (block.sync, block.vector, block.scalar,
                           block.tensor, block.gpsimd):
                def f(e, snap=snap):
                    for sem, v in snap:
                        e.wait_ge(sem, v)
                opener(f)

        def init_g(g):
            G.inc(g.iota(scratch_f[:, 0:128], [[-1, 128]], base=0,
                         channel_multiplier=1,
                         allow_small_or_imprecise_dtypes=True))
        block.gpsimd(init_g)
        g1 = G.v

        def init_v1(v):
            v.wait_ge(G.sem, g1)
            V.inc(v.tensor_scalar(out=ident[:], in0=scratch_f[:, 0:128],
                                  scalar1=0.0, scalar2=None, op0=ALU.is_equal))
            V.inc(v.memset(ones1[:], 1.0))
        block.vector(init_v1)
        v1 = V.v

        def init_g2(g):
            g.wait_ge(V.sem, v1)
            G.inc(g.iota(scratch_f[:], [[1, 172]], base=-86,
                         channel_multiplier=0,
                         allow_small_or_imprecise_dtypes=True))
        block.gpsimd(init_g2)
        g2 = G.v

        def init_v2(v):
            v.wait_ge(G.sem, g2)
            V.inc(v.tensor_scalar(out=indneg[:], in0=scratch_f[:], scalar1=0.0,
                                  scalar2=-1.0, op0=ALU.is_equal, op1=ALU.mult))
        block.vector(init_v2)

        with nc.sbuf_tensor("qzb", [32, J // 8], U32) as qzb, \
             nc.sbuf_tensor("zqt", [32, J // 4], U32) as zqt, \
             nc.sbuf_tensor("scb", [32, J // 4], F32) as scb, \
             nc.sbuf_tensor("bgu_x", [32, J // 4], BF16) as bgu_x, \
             nc.sbuf_tensor("qzdb", [86, D // 8], U32) as qzdb, \
             nc.sbuf_tensor("zqdt", [86, D], U32) as zqdt, \
             nc.sbuf_tensor("scdb", [86, D], F32) as scdb, \
             nc.sbuf_tensor("bd_x", [86, D], BF16) as bd_x:
            CJ = J // 4
            CP = J // 32

            def p1a_l0(sy):
                SQ.inc(sy.dma_start(out=qzb[:], in_=qz_gu.ap()))
                SQ.inc(sy.dma_start(out=qzdb[:], in_=qz_d.ap()))
            block.sync(p1a_l0)
            ld0 = SQ.v

            spill_prev = [0]
            vtok_prev = [0]
            for cc in range(4):
                def p1a_ls(sy, cc=cc):
                    if vtok_prev[0]:
                        sy.wait_ge(V.sem, vtok_prev[0])
                    SQ.inc(sy.dma_start(
                        out=scb[:], in_=sc_gu.ap()[:, cc * CJ:(cc + 1) * CJ]))
                block.sync(p1a_ls)
                scld = SQ.v

                def p1a_v(v, cc=cc, scld=scld):
                    v.wait_ge(SQ.sem, max(ld0, scld))
                    if spill_prev[0]:
                        v.wait_ge(AQ.sem, spill_prev[0])
                    for k in range(8):
                        V.inc(v.tensor_scalar(
                            out=bass.AP(zqt, k, [[CJ, 32], [8, CP]]),
                            in0=qzb[:, cc * CP:(cc + 1) * CP],
                            scalar1=4 * k, scalar2=15,
                            op0=ALU.logical_shift_right, op1=ALU.bitwise_and))
                    v.wait_ge(V.sem, V.v)
                    V.inc(v.tensor_scalar(out=zqt[:], in0=zqt[:], scalar1=1,
                                          scalar2=None, op0=ALU.add))
                    v.wait_ge(V.sem, V.v)
                    V.inc(v.tensor_tensor(out=bgu_x[:], in0=zqt[:], in1=scb[:],
                                          op=ALU.mult))
                block.vector(p1a_v)
                vtok_prev[0] = V.v
                vtok = V.v

                def p1a_s(sc, cc=cc, vtok=vtok):
                    sc.wait_ge(V.sem, vtok)
                    AQ.inc(sc.dma_start(out=bgu_ap[:, cc * CJ:(cc + 1) * CJ],
                                        in_=bgu_x[:]))
                block.scalar(p1a_s)
                spill_prev[0] = AQ.v

            def p1a_ld(sy):
                SCD.inc(sy.dma_start(out=scdb[:], in_=sc_d.ap()))
            block.sync(p1a_ld)
            scdld = SCD.v

            def p1a_vd(v, scdld=scdld):
                v.wait_ge(SCD.sem, scdld)
                for k in range(8):
                    V.inc(v.tensor_scalar(
                        out=bass.AP(zqdt, k, [[D, 86], [8, D // 8]]),
                        in0=qzdb[:], scalar1=4 * k, scalar2=15,
                        op0=ALU.logical_shift_right, op1=ALU.bitwise_and))
                v.wait_ge(V.sem, V.v)
                V.inc(v.tensor_scalar(out=zqdt[:], in0=zqdt[:], scalar1=1,
                                      scalar2=None, op0=ALU.add))
                v.wait_ge(V.sem, V.v)
                V.inc(v.tensor_tensor(out=bd_x[:], in0=zqdt[:], in1=scdb[:],
                                      op=ALU.mult))
            block.vector(p1a_vd)
            vdt = V.v

            def p1a_sd(sc, vdt=vdt):
                sc.wait_ge(V.sem, vdt)
                AQ.inc(sc.dma_start(out=bd_ap, in_=bd_x[:]))
            block.scalar(p1a_sd)

            barrier()

        with nc.sbuf_tensor("xnT", [128, NKT, T], BF16) as xnT:

            with nc.sbuf_tensor("xst", [128, D], F32) as xst, \
                 nc.sbuf_tensor("p1scrap", [128, D // 2], F32) as p1scrap, \
                 nc.sbuf_tensor("xnbf", [128, D], BF16) as xnbf, \
                 nc.sbuf_tensor("xnpm", [128, D], BF16) as xnpm, \
                 nc.sbuf_tensor("p1tmp", [128, 8], F32) as p1tmp, \
                 nc.sbuf_tensor("nrspf", [128, 32], F32) as nrspf, \
                 nc.sbuf_tensor("nrsp", [128, 32], BF16) as nrsp, \
                 nc.psum_tensor("pstr0", [128, 128], BF16) as pstr0, \
                 nc.psum_tensor("pstr1", [128, 128], BF16) as pstr1, \
                 nc.psum_tensor("psnrs", [32, 128], BF16) as psnrs:
                pstr = [pstr0, pstr1]
                prev = {"sxn": 0, "nrse": 0, "vvar": 0}
                for tt in range(NTT):
                    def p1_l(sy, tt=tt):
                        if prev["sxn"]:
                            sy.wait_ge(S.sem, prev["sxn"])
                        if prev["nrse"]:
                            sy.wait_ge(V.sem, prev["nrse"])
                        SQ.inc(sy.dma_start(
                            out=xst[:], in_=xtk[tt * 128:(tt + 1) * 128, :]))
                    block.sync(p1_l)
                    xld = SQ.v

                    def p1_v1(v, xld=xld):
                        v.wait_ge(SQ.sem, xld)
                        V.inc(v.reduce_sum(out=p1tmp[:, 0:1], in_=xst[:, :D // 2],
                                           axis=AX.X))
                        V.inc(v.reduce_sum(out=p1tmp[:, 1:2], in_=xst[:, D // 2:],
                                           axis=AX.X))
                    block.vector(p1_v1)
                    vsum = V.v

                    def p1_s1(sc, xld=xld):
                        sc.wait_ge(SQ.sem, xld)
                        if prev["vvar"]:
                            sc.wait_ge(V.sem, prev["vvar"])
                        S.inc(sc.activation(out=p1scrap[:], in_=xst[:, :D // 2],
                                            func=AF.Square,
                                            accum_out=p1tmp[:, 2:3]))
                        sc.wait_ge(S.sem, S.v)
                        S.inc(sc.activation(out=p1scrap[:], in_=xst[:, D // 2:],
                                            func=AF.Square,
                                            accum_out=p1tmp[:, 3:4]))
                    block.scalar(p1_s1)
                    ssq = S.v

                    def p1_v2(v, tt=tt, vsum=vsum, ssq=ssq):
                        v.wait_ge(V.sem, vsum)
                        v.wait_ge(S.sem, ssq)
                        V.inc(v.tensor_scalar(out=stats_b[:, tt, 0:1],
                                              in0=p1tmp[:, 0:1],
                                              scalar1=p1tmp[:, 1:2],
                                              scalar2=1.0 / D,
                                              op0=ALU.add, op1=ALU.mult))
                        V.inc(v.tensor_scalar(out=p1tmp[:, 5:6],
                                              in0=p1tmp[:, 2:3],
                                              scalar1=p1tmp[:, 3:4],
                                              scalar2=1.0 / D,
                                              op0=ALU.add, op1=ALU.mult))
                        v.wait_ge(V.sem, V.v)
                        V.inc(v.tensor_tensor(out=p1tmp[:, 6:7],
                                              in0=stats_b[:, tt, 0:1],
                                              in1=stats_b[:, tt, 0:1],
                                              op=ALU.mult))
                        v.wait_ge(V.sem, V.v)
                        V.inc(v.tensor_scalar(out=p1tmp[:, 5:6],
                                              in0=p1tmp[:, 5:6],
                                              scalar1=p1tmp[:, 6:7], scalar2=EPS,
                                              op0=ALU.subtract, op1=ALU.add))
                    block.vector(p1_v2)
                    vvar = V.v
                    prev["vvar"] = vvar

                    def p1_s2(sc, vvar=vvar):
                        sc.wait_ge(V.sem, vvar)
                        S.inc(sc.activation(out=p1tmp[:, 7:8], in_=p1tmp[:, 5:6],
                                            func=AF.Sqrt))
                    block.scalar(p1_s2)
                    ssd = S.v

                    def p1_v3(v, tt=tt, ssd=ssd):
                        v.wait_ge(S.sem, ssd)
                        V.inc(v.reciprocal(out=stats_b[:, tt, 1:2],
                                           in_=p1tmp[:, 7:8]))
                        v.wait_ge(V.sem, V.v)
                        V.inc(v.tensor_scalar(out=stats_b[:, tt, 2:3],
                                              in0=stats_b[:, tt, 0:1],
                                              scalar1=stats_b[:, tt, 1:2],
                                              scalar2=-1.0, op0=ALU.mult,
                                              op1=ALU.mult))
                    block.vector(p1_v3)
                    vst = V.v

                    def p1_s3(sc, tt=tt, vst=vst):
                        sc.wait_ge(V.sem, vst)
                        if prev["nrse"]:
                            sc.wait_ge(V.sem, prev["nrse"])
                        S.inc(sc.activation(out=xnbf[:], in_=xst[:],
                                            func=AF.Identity,
                                            scale=stats_b[:, tt, 1:2],
                                            bias=stats_b[:, tt, 2:3]))
                    block.scalar(p1_s3)
                    sxn = S.v
                    prev["sxn"] = sxn

                    def p1_v4(v, tt=tt, sxn=sxn):
                        v.wait_ge(S.sem, sxn)
                        if prev["nrse"]:
                            v.wait_ge(V.sem, prev["nrse"])
                        V.inc(v.tensor_copy(
                            out=xnpm[:],
                            in_=bass.AP(xnbf, 0,
                                        [[D, 128], [1024, 4], [1, 8], [8, 128]])))
                        V.inc(v.tensor_reduce(
                            out=nrspf[:],
                            in_=bass.AP(xnbf, 0, [[D, 128], [128, 32], [1, 128]]),
                            axis=AX.X, op=ALU.add, negate=True))
                        v.wait_ge(V.sem, V.v)
                        V.inc(v.tensor_copy(out=nrsp[:], in_=nrspf[:]))
                    block.vector(p1_v4)
                    vperm = V.v

                    evac_base = V.v

                    def p1_t(te, tt=tt, vperm=vperm, evac_base=evac_base):
                        te.wait_ge(V.sem, vperm)
                        for kt in range(NKT):
                            if kt >= 2:
                                te.wait_ge(V.sem, evac_base + kt - 1)
                            TE.inc(te.transpose(pstr[kt % 2][:],
                                                xnpm[:, kt * 128:(kt + 1) * 128],
                                                ident[:]))
                        TE.inc(te.transpose(psnrs[:], nrsp[:], ident[:]))
                    tr_base = TE.v
                    block.tensor(p1_t)

                    def p1_v5(v, tt=tt, tr_base=tr_base):
                        for kt in range(NKT):
                            v.wait_ge(TE.sem, tr_base + kt + 1)
                            V.inc(v.tensor_copy(
                                out=xnT[:, kt, tt * 128:(tt + 1) * 128],
                                in_=pstr[kt % 2][:]))
                        v.wait_ge(TE.sem, tr_base + NKT + 1)
                        V.inc(v.tensor_copy(out=nrs_t[:, tt * 128:(tt + 1) * 128],
                                            in_=psnrs[:]))
                    block.vector(p1_v5)
                    assert V.v == evac_base + NKT + 1
                    prev["nrse"] = V.v

            barrier()

            with nc.sbuf_tensor("xst0", [128, 4, D // 2], F32) as xst0, \
                 nc.sbuf_tensor("p0scrap", [128, D // 2], F32) as p0scrap, \
                 nc.sbuf_tensor("p0sum", [128, NTT, 2], F32) as p0sum, \
                 nc.sbuf_tensor("p0sq", [128, NTT, 2], F32) as p0sq, \
                 nc.sbuf_tensor("p0t", [128, NTT, 4], F32) as p0t:
                prev_v3 = [0]
                for m in range(B8):
                    vb = V.v
                    sbase = S.v
                    p0toks = []

                    def p0_l(sy, m=m, vb=vb, sbase=sbase, p0toks=p0toks):
                        for i in range(32):
                            if i >= 4:
                                sy.wait_ge(V.sem, vb + i - 3)
                                sy.wait_ge(S.sem, sbase + i - 3)
                            elif m > 0:
                                sy.wait_ge(V.sem, vb)
                                sy.wait_ge(S.sem, sbase)
                            tt, hh = divmod(i, 2)
                            p0toks.append(P0L[i % 4].inc(sy.dma_start(
                                out=xst0[:, i % 4, :],
                                in_=xap[m, tt * 128:(tt + 1) * 128,
                                        hh * (D // 2):(hh + 1) * (D // 2)])))
                    block.sync(p0_l)

                    def p0_v1(v, m=m, p0toks=p0toks):
                        if prev_v3[0]:
                            v.wait_ge(V.sem, prev_v3[0])
                        for i in range(32):
                            tt, hh = divmod(i, 2)
                            v.wait_ge(P0L[i % 4].sem, p0toks[i])
                            V.inc(v.reduce_sum(out=p0sum[:, tt, hh:hh + 1],
                                               in_=xst0[:, i % 4, :], axis=AX.X))
                    block.vector(p0_v1)
                    assert V.v == vb + 32

                    def p0_s1(sc, m=m, p0toks=p0toks):
                        if prev_v3[0]:
                            sc.wait_ge(V.sem, prev_v3[0])
                        for i in range(32):
                            tt, hh = divmod(i, 2)
                            sc.wait_ge(P0L[i % 4].sem, p0toks[i])
                            S.inc(sc.activation(out=p0scrap[:],
                                                in_=xst0[:, i % 4, :],
                                                func=AF.Square,
                                                accum_out=p0sq[:, tt, hh:hh + 1]))
                    block.scalar(p0_s1)
                    assert S.v == sbase + 32
                    s1 = S.v
                    v1b = V.v

                    def p0_v2(v, m=m, s1=s1, v1b=v1b):
                        v.wait_ge(V.sem, v1b)
                        v.wait_ge(S.sem, s1)
                        for tt in range(NTT):
                            V.inc(v.tensor_scalar(
                                out=stats_all[:, m, tt, 0:1],
                                in0=p0sum[:, tt, 0:1], scalar1=p0sum[:, tt, 1:2],
                                scalar2=1.0 / D, op0=ALU.add, op1=ALU.mult))
                            V.inc(v.tensor_scalar(
                                out=p0t[:, tt, 0:1],
                                in0=p0sq[:, tt, 0:1], scalar1=p0sq[:, tt, 1:2],
                                scalar2=1.0 / D, op0=ALU.add, op1=ALU.mult))
                        v.wait_ge(V.sem, V.v)
                        for tt in range(NTT):
                            V.inc(v.tensor_tensor(out=p0t[:, tt, 1:2],
                                                  in0=stats_all[:, m, tt, 0:1],
                                                  in1=stats_all[:, m, tt, 0:1],
                                                  op=ALU.mult))
                        v.wait_ge(V.sem, V.v)
                        for tt in range(NTT):
                            V.inc(v.tensor_scalar(out=p0t[:, tt, 2:3],
                                                  in0=p0t[:, tt, 0:1],
                                                  scalar1=p0t[:, tt, 1:2],
                                                  scalar2=EPS, op0=ALU.subtract,
                                                  op1=ALU.add))
                    block.vector(p0_v2)
                    v2 = V.v

                    def p0_s2(sc, m=m, v2=v2):
                        sc.wait_ge(V.sem, v2)
                        for tt in range(NTT):
                            S.inc(sc.activation(out=p0t[:, tt, 3:4],
                                                in_=p0t[:, tt, 2:3],
                                                func=AF.Sqrt))
                    block.scalar(p0_s2)
                    s2 = S.v

                    def p0_v3(v, m=m, s2=s2):
                        v.wait_ge(S.sem, s2)
                        for tt in range(NTT):
                            V.inc(v.reciprocal(out=stats_all[:, m, tt, 1:2],
                                               in_=p0t[:, tt, 3:4]))
                        v.wait_ge(V.sem, V.v)
                        for tt in range(NTT):
                            V.inc(v.tensor_scalar(out=stats_all[:, m, tt, 2:3],
                                                  in0=stats_all[:, m, tt, 0:1],
                                                  scalar1=stats_all[:, m, tt, 1:2],
                                                  scalar2=-1.0, op0=ALU.mult,
                                                  op1=ALU.mult))
                    block.vector(p0_v3)
                    prev_v3[0] = V.v

            barrier()

            with nc.sbuf_tensor("xq", [128, B8, NTT, 128], BF16) as xq, \
                 nc.sbuf_tensor("xqb", [128, NTT, 128], BF16) as xqb, \
                 nc.sbuf_tensor("xst2", [128, 2, NTT, 128], F32) as xst2, \
                 nc.sbuf_tensor("macc", [128, NTT, 128], BF16) as macc, \
                 nc.sbuf_tensor("mprod", [128, NTT, 128], BF16) as mprod, \
                 nc.sbuf_tensor("msk", [128, 128], BF16) as msk, \
                 nc.sbuf_tensor("et_sb", [1, 1024], BF16) as et_sb, \
                 nc.sbuf_tensor("erep", [128, 1024], BF16) as erep, \
                 nc.sbuf_tensor("p2t", [128, 4], F32) as p2t, \
                 nc.sbuf_tensor("e_f", [128, 8], F32) as e_f, \
                 nc.psum_tensor("gps0", [128, 128], F32) as gps0, \
                 nc.psum_tensor("gps1", [128, 128], F32) as gps1, \
                 nc.psum_tensor("etps", [1, 1024], BF16) as etps, \
                 nc.psum_tensor("erps0", [128, 512], F32) as erps0, \
                 nc.psum_tensor("erps1", [128, 512], F32) as erps1:
                gps = [gps0, gps1]
                erps = [erps0, erps1]
                pc = {"gram_last": 0, "diag_last": 0, "spill": 0,
                      "msg_last": 0, "ete": 0, "erepe": 0, "norm_last": 0,
                      "exp": 0}
                for ch in range(NCH):
                    norm_toks = []
                    for m in range(B8 + 1):
                        def p2_l(sy, ch=ch, m=m):
                            if m >= 2:
                                sy.wait_ge(V.sem, norm_toks[m - 2])
                            elif pc["norm_last"]:
                                sy.wait_ge(V.sem, pc["norm_last"])
                            if m < B8:
                                src = xap[m].rearrange("(ct p) l -> p ct l",
                                                       p=128)[:, :,
                                                              ch * 128:(ch + 1) * 128]
                                P2L[m % 2].inc(sy.dma_start(out=xst2[:, m % 2],
                                                            in_=src))
                            else:
                                src = xtk.rearrange("(ct p) l -> p ct l",
                                                    p=128)[:, :,
                                                           ch * 128:(ch + 1) * 128]
                                P2L[m % 2].inc(sy.dma_start(out=xst2[:, m % 2],
                                                            in_=src))
                        block.sync(p2_l)
                        lt = P2L[m % 2].v

                        def p2_n(v, ch=ch, m=m, lt=lt):
                            v.wait_ge(P2L[m % 2].sem, lt)
                            if m == 0 and pc["gram_last"]:
                                v.wait_ge(TE.sem, pc["gram_last"])
                            st = stats_all[:, m] if m < B8 else stats_b
                            for ct in range(NTT):
                                dst = xq[:, m, ct, :] if m < B8 else xqb[:, ct, :]
                                V.inc(v.tensor_scalar(
                                    out=dst, in0=xst2[:, m % 2, ct, :],
                                    scalar1=st[:, ct, 1:2],
                                    scalar2=st[:, ct, 2:3],
                                    op0=ALU.mult, op1=ALU.add))
                        block.vector(p2_n)
                        norm_toks.append(V.v)
                    pc["norm_last"] = V.v

                    diag_pred = V.v

                    def p2_g(te, ch=ch, diag_pred=diag_pred):
                        te.wait_ge(V.sem, pc["norm_last"])
                        for m in range(B8):
                            if m >= 2:
                                te.wait_ge(V.sem, diag_pred + 2 * (m - 1))
                            elif pc["diag_last"]:
                                te.wait_ge(V.sem, pc["diag_last"])
                            for ct in range(NTT):
                                mm = te.matmul(gps[m % 2][:], xqb[:, ct, :],
                                               xq[:, m, ct, :],
                                               start=(ct == 0),
                                               stop=(ct == NTT - 1))
                                if ct == NTT - 1:
                                    TE.inc(mm)
                    gram_base = TE.v
                    block.tensor(p2_g)
                    pc["gram_last"] = TE.v

                    def p2_d(v, ch=ch, gram_base=gram_base):
                        if pc["exp"]:
                            v.wait_ge(S.sem, pc["exp"])
                        for m in range(B8):
                            v.wait_ge(TE.sem, gram_base + m + 1)
                            V.inc(v.tensor_tensor(out=msk[:], in0=gps[m % 2][:],
                                                  in1=ident[:], op=ALU.mult))
                            v.wait_ge(V.sem, V.v)
                            V.inc(v.reduce_sum(out=S_all[:, ch, m:m + 1],
                                               in_=msk[:], axis=AX.X))
                        v.wait_ge(V.sem, V.v)
                        V.inc(v.reduce_max(out=p2t[:, 0:1], in_=S_all[:, ch, :],
                                           axis=AX.X))
                        v.wait_ge(V.sem, V.v)
                        V.inc(v.tensor_scalar(out=p2t[:, 1:2], in0=p2t[:, 0:1],
                                              scalar1=-INVS, scalar2=None,
                                              op0=ALU.mult))
                    block.vector(p2_d)
                    assert V.v == diag_pred + 2 * B8 + 2
                    pc["diag_last"] = V.v
                    vdiag = V.v

                    def p2_e(sc, ch=ch, vdiag=vdiag):
                        sc.wait_ge(V.sem, vdiag)
                        S.inc(sc.activation(out=e_f[:], in_=S_all[:, ch, :],
                                            func=AF.Exp, scale=INVS,
                                            bias=p2t[:, 1:2]))
                    block.scalar(p2_e)
                    pc["exp"] = S.v
                    sexp = S.v

                    def p2_sm(v, ch=ch, sexp=sexp):
                        v.wait_ge(S.sem, sexp)
                        V.inc(v.reduce_sum(out=p2t[:, 2:3], in_=e_f[:],
                                           axis=AX.X))
                        v.wait_ge(V.sem, V.v)
                        V.inc(v.reciprocal(out=p2t[:, 3:4], in_=p2t[:, 2:3]))
                        v.wait_ge(V.sem, V.v)
                        V.inc(v.tensor_scalar(out=E_all[:, ch, :], in0=e_f[:],
                                              scalar1=p2t[:, 3:4], scalar2=None,
                                              op0=ALU.mult))
                    block.vector(p2_sm)
                    ven = V.v

                    def p2_t1(te, ch=ch, ven=ven):
                        te.wait_ge(V.sem, ven)
                        if pc["ete"]:
                            te.wait_ge(V.sem, pc["ete"])
                        for m in range(B8):
                            TE.inc(te.transpose(etps[0:1, m * 128:(m + 1) * 128],
                                                E_all[:, ch, m:m + 1], ident[:]))
                    block.tensor(p2_t1)
                    ttr = TE.v

                    def p2_ev(v, ch=ch, ttr=ttr):
                        v.wait_ge(TE.sem, ttr)
                        V.inc(v.tensor_copy(out=et_sb[:], in_=etps[:]))
                    block.vector(p2_ev)
                    pc["ete"] = V.v
                    vet = V.v

                    def p2_t2(te, ch=ch, vet=vet):
                        te.wait_ge(V.sem, vet)
                        if pc["erepe"]:
                            te.wait_ge(V.sem, pc["erepe"])
                        for m in range(B8):
                            TE.inc(te.matmul(
                                erps[m // 4][:, (m % 4) * 128:(m % 4 + 1) * 128],
                                ones1[:], et_sb[0:1, m * 128:(m + 1) * 128],
                                start=True, stop=True))
                    block.tensor(p2_t2)
                    terep = TE.v

                    def p2_m(v, ch=ch, terep=terep):
                        v.wait_ge(TE.sem, terep)
                        if pc["spill"]:
                            v.wait_ge(AQ.sem, pc["spill"])
                        V.inc(v.tensor_copy(out=erep[:, 0:512], in_=erps[0][:]))
                        V.inc(v.tensor_copy(out=erep[:, 512:1024],
                                            in_=erps[1][:]))
                        v.wait_ge(V.sem, V.v)
                        pc["erepe"] = V.v
                        for m in range(B8):
                            bcast = bass.AP(erep, m * 128,
                                            [[1024, 128], [0, NTT], [1, 128]])
                            if m == 0:
                                V.inc(v.tensor_tensor(out=macc[:], in0=xq[:, 0],
                                                      in1=bcast, op=ALU.mult))
                            else:
                                V.inc(v.tensor_tensor(out=mprod[:],
                                                      in0=xq[:, m], in1=bcast,
                                                      op=ALU.mult))
                                v.wait_ge(V.sem, V.v)
                                V.inc(v.tensor_tensor(out=macc[:], in0=macc[:],
                                                      in1=mprod[:], op=ALU.add))
                            v.wait_ge(V.sem, V.v)
                    block.vector(p2_m)
                    pc["msg_last"] = V.v
                    vmsg = V.v

                    def p2_sp(sc, ch=ch, vmsg=vmsg):
                        sc.wait_ge(V.sem, vmsg)
                        dst = bass.AP(msg_dram, ch * 128,
                                      [[D, 128], [128 * D, NTT], [1, 128]])
                        AQ.inc(sc.dma_start(out=dst, in_=macc[:]))
                    block.scalar(p2_sp)
                    pc["spill"] = AQ.v

            barrier()

            with nc.sbuf_tensor("wring", [128, 2, 2, 2, 4, 4, 128], BF16) as wring, \
                 nc.sbuf_tensor("pkr", [128, 2, 2, 2048], U8) as pkr, \
                 nc.sbuf_tensor("srr", [128, 2, 2, 4, 128], F32) as srr, \
                 nc.sbuf_tensor("bgur", [32, 2, 2, 128], BF16) as bgur, \
                 nc.sbuf_tensor("nl3", [128, 2048], U8) as nl3, \
                 nc.sbuf_tensor("nh3", [128, 2048], U8) as nh3, \
                 nc.sbuf_tensor("hstage", [128, 8, 512], BF16) as hstage, \
                 nc.sbuf_tensor("sig", [128, 2, 512], F32) as sig, \
                 nc.psum_tensor("pg0", [128, 512], F32) as pg0, \
                 nc.psum_tensor("pg1", [128, 512], F32) as pg1, \
                 nc.psum_tensor("pu0", [128, 512], F32) as pu0, \
                 nc.psum_tensor("pu1", [128, 512], F32) as pu1, \
                 nc.psum_tensor("rs0", [86, 512], F32) as rs0, \
                 nc.psum_tensor("rs1", [86, 512], F32) as rs1, \
                 nc.psum_tensor("rs2", [86, 512], F32) as rs2, \
                 nc.psum_tensor("rs3", [86, 512], F32) as rs3:
                pg = [pg0, pg1]
                pu = [pu0, pu1]
                rsp = [rs0, rs1, rs2, rs3]

                def p3_loads(sy, a, war_tok, te_tok):
                    if war_tok:
                        sy.wait_ge(V.sem, war_tok)
                    if te_tok:
                        sy.wait_ge(TE.sem, te_tok)
                    sl = a % 2
                    qp = qprep.ap()
                    sp = srprep.ap()
                    for s, base in ((0, 0), (1, FF)):
                        off = base + a * 128
                        P3L[sl].inc(sy.dma_start(out=pkr[:, sl, s],
                                                 in_=qp[a, s]))
                        P3L[sl].inc(sy.dma_start(out=srr[:, sl, s],
                                                 in_=sp[a, s]))
                        P3L[sl].inc(sy.dma_start(out=bgur[:, sl, s],
                                                 in_=bgu_ap[:, off:off + 128]))

                block.sync(lambda sy: p3_loads(sy, 0, 0, 0))
                ld_tok = {0: P3L[0].v}
                deq_tok = {}
                hT_tok = {}
                silu_tok = {}
                spill_tok3 = {}
                mm_tok = {}

                for a in range(NA):
                    def p3_l(sy, a=a):
                        if a + 1 < NA:
                            p3_loads(sy, a + 1, deq_tok.get(a - 1, 0),
                                     mm_tok.get(a - 1, 0))
                    block.sync(p3_l)
                    if a + 1 < NA:
                        ld_tok[a + 1] = P3L[(a + 1) % 2].v

                    def p3_dq(v, a=a):
                        v.wait_ge(P3L[a % 2].sem, ld_tok[a])
                        if a >= 2:
                            v.wait_ge(TE.sem, mm_tok[a - 2])
                        sl = a % 2
                        for s in range(2):
                            V.inc(v.tensor_scalar(
                                out=nl3[:], in0=pkr[:, sl, s, :],
                                scalar1=15, scalar2=None, op0=ALU.bitwise_and))
                            V.inc(v.tensor_scalar(
                                out=nh3[:], in0=pkr[:, sl, s, :],
                                scalar1=4, scalar2=None,
                                op0=ALU.logical_shift_right))
                            v.wait_ge(V.sem, V.v)
                            in1 = bass.AP(srr, (sl * 2 + s) * 512,
                                          [[2048, 128], [128, 4],
                                           [0, 4], [1, 128]])
                            V.inc(v.tensor_tensor(
                                out=wring[:, sl, s, 0], in0=nl3[:],
                                in1=in1, op=ALU.mult))
                            V.inc(v.tensor_tensor(
                                out=wring[:, sl, s, 1], in0=nh3[:],
                                in1=in1, op=ALU.mult))
                            v.wait_ge(V.sem, V.v)
                    block.vector(p3_dq)
                    deq_tok[a] = V.v
                    vpred = V.v
                    silu_pred = S.v
                    mmb = TE.v

                    def p3_mm(te, a=a, vpred=vpred, silu_pred=silu_pred):
                        te.wait_ge(V.sem, deq_tok[a])
                        sl = a % 2
                        for tc in range(NTC):
                            if tc >= 2:
                                te.wait_ge(S.sem, silu_pred + tc - 1)
                                te.wait_ge(V.sem, vpred + tc - 1)
                            elif a >= 1:
                                te.wait_ge(S.sem, silu_tok[a - 1])
                                te.wait_ge(V.sem, hT_tok[a - 1])
                            for s, bank in ((0, pg[tc % 2]), (1, pu[tc % 2])):
                                for kt in range(NKT):
                                    wsl = wring[:, sl, s, kt % 2, kt // 8,
                                                (kt % 8) // 2, :]
                                    te.matmul(bank[:], wsl,
                                              xnT[:, kt, tc * 512:(tc + 1) * 512],
                                              start=(kt == 0), stop=False)
                                TE.inc(te.matmul(
                                    bank[:], bgur[:, sl, s],
                                    nrs_t[:, tc * 512:(tc + 1) * 512],
                                    start=False, stop=True))
                        if a >= 1:
                            for tc in range(NTC):
                                slot = ((a - 1) * NTC + tc) % 8
                                TE.inc(te.matmul(
                                    rsp[tc][:],
                                    indneg[:, 86 - (a - 1):172 - (a - 1)],
                                    hstage[:, slot, :],
                                    start=(a - 1 == 0), stop=False))
                    block.tensor(p3_mm)
                    mm_tok[a] = TE.v

                    def p3_si(sc, a=a, mmb=mmb, vpred=vpred):
                        for tc in range(NTC):
                            sc.wait_ge(TE.sem, mmb + 2 * tc + 1)
                            if tc >= 2:
                                sc.wait_ge(V.sem, vpred + tc - 1)
                            elif a >= 1:
                                sc.wait_ge(V.sem, hT_tok[a - 1])
                            S.inc(sc.activation(out=sig[:, tc % 2, :],
                                                in_=pg[tc % 2][:],
                                                func=AF.Silu))
                    block.scalar(p3_si)
                    assert S.v == silu_pred + NTC
                    silu_tok[a] = S.v

                    def p3_h(v, a=a, mmb=mmb, silu_pred=silu_pred):
                        for tc in range(NTC):
                            slot = (a * NTC + tc) % 8
                            v.wait_ge(S.sem, silu_pred + tc + 1)
                            v.wait_ge(TE.sem, mmb + 2 * tc + 2)
                            if spill_tok3.get(a - 2) is not None:
                                v.wait_ge(SP3[a % 2].sem, spill_tok3[a - 2])
                            V.inc(v.tensor_tensor(out=hstage[:, slot, :],
                                                  in0=sig[:, tc % 2, :],
                                                  in1=pu[tc % 2][:],
                                                  op=ALU.mult))
                    block.vector(p3_h)
                    assert V.v == vpred + NTC
                    hT_tok[a] = V.v
                    hbase = V.v - NTC

                    def p3_sp(sc, a=a, hbase=hbase):
                        for tc in range(NTC):
                            slot = (a * NTC + tc) % 8
                            sc.wait_ge(V.sem, hbase + tc + 1)
                            if a < 80:
                                roff = 1024 * (a // 8) + 16 * (a % 8)
                                dst = bass.AP(h_dram, roff * T + tc * 512,
                                              [[T, 16], [128 * T, 8], [1, 512]])
                            else:
                                roff = 10240 + 16 * (a - 80)
                                dst = bass.AP(h_dram, roff * T + tc * 512,
                                              [[T, 16], [96 * T, 8], [1, 512]])
                            SP3[a % 2].inc(sc.dma_start(out=dst,
                                                        in_=hstage[:, slot, :]))
                        spill_tok3[a] = SP3[a % 2].v
                    block.scalar(p3_sp)

                def p3_ep_t(te):
                    a = NA - 1
                    te.wait_ge(V.sem, hT_tok[a])
                    for tc in range(NTC):
                        slot = (a * NTC + tc) % 8
                        TE.inc(te.matmul(rsp[tc][:], indneg[:, 86 - a:172 - a],
                                         hstage[:, slot, :], start=False,
                                         stop=True))
                block.tensor(p3_ep_t)
                rs_done = TE.v

                def p3_ep_v(v, rs_done=rs_done):
                    v.wait_ge(TE.sem, rs_done)
                    for tc in range(NTC):
                        V.inc(v.tensor_copy(
                            out=nrsh_t[:, tc * 512:(tc + 1) * 512],
                            in_=rsp[tc][:]))
                block.vector(p3_ep_v)

                barrier()

        with nc.sbuf_tensor("wd", [128, 2, 11, 4, 512], BF16) as wd, \
             nc.sbuf_tensor("pk4", [128, 2, 2048], U8) as pk4, \
             nc.sbuf_tensor("sr4", [128, 2, 512], F32) as sr4, \
             nc.sbuf_tensor("nl4", [128, 2048], U8) as nl4, \
             nc.sbuf_tensor("nh4", [128, 2048], U8) as nh4, \
             nc.sbuf_tensor("bd4", [86, 512], BF16) as bd4, \
             nc.sbuf_tensor("hbuf", [128, 2, 88, 128], BF16) as hbuf, \
             nc.sbuf_tensor("xpc", [128, 2, 512], F32) as xpc, \
             nc.sbuf_tensor("xnpc", [128, 2, 512], F32) as xnpc, \
             nc.sbuf_tensor("msgpc", [128, 2, 512], BF16) as msgpc, \
             nc.sbuf_tensor("osb", [128, 2, 512], F32) as osb, \
             nc.psum_tensor("od0", [128, 512], F32) as od0, \
             nc.psum_tensor("od1", [128, 512], F32) as od1:
            od = [od0, od1]

            def p4_hload(sy, d, tt):
                par = tt % 2
                P4H[par].inc(sy.dma_start(
                    out=hbuf[:, par, 0:80, :],
                    in_=bass.AP(h_dram, tt * 128,
                                [[T, 128], [128 * T, 80], [1, 128]])))
                P4H[par].inc(sy.dma_start(
                    out=hbuf[0:96, par, 80:88, :],
                    in_=bass.AP(h_dram, 10240 * T + tt * 128,
                                [[T, 96], [96 * T, 8], [1, 128]])))
                P4H[par].inc(sy.dma_start(
                    out=xpc[:, par, :],
                    in_=xtk[tt * 128:(tt + 1) * 128, d * 512:(d + 1) * 512]))
                P4H[par].inc(sy.dma_start(
                    out=msgpc[:, par, :],
                    in_=msg_ap[tt * 128:(tt + 1) * 128,
                               d * 512:(d + 1) * 512]))
                return P4H[par].v

            mm4_tok = {}
            add_tok = {}
            st_tok = {}
            dq4_last = [0]
            for d in range(8):
                vb4 = V.v
                wtok = {}
                ld4 = {}

                def p4_l(sy, d=d, vb4=vb4, wtok=wtok, ld4=ld4):
                    if d > 0:
                        sy.wait_ge(TE.sem, mm4_tok[(d - 1, NTT - 1)])
                        sy.wait_ge(V.sem, add_tok[(d - 1, NTT - 1)])
                    BDC.inc(sy.dma_start(out=bd4[:],
                                         in_=bd_ap[:, d * 512:(d + 1) * 512]))
                    for blk in range(11):
                        if blk >= 2:
                            sy.wait_ge(V.sem, vb4 + 4 * (blk - 1))
                        kp = 128 if blk < 10 else 96
                        P4W[blk % 2].inc(sy.dma_start(
                            out=pk4[0:kp, blk % 2, :],
                            in_=qdprep.ap()[blk, 0:kp, :,
                                            d * 512:(d + 1) * 512]))
                        wtok[blk] = P4W[blk % 2].inc(sy.dma_start(
                            out=sr4[0:kp, blk % 2, :],
                            in_=bass.AP(srep_d, (128 * blk) * D + d * 512,
                                        [[D, kp], [1, 512]])))
                    ld4[0] = p4_hload(sy, d, 0)
                block.sync(p4_l)
                bdtok = BDC.v

                def p4_dq(v, d=d, wtok=wtok):
                    if d > 0:
                        v.wait_ge(TE.sem, mm4_tok[(d - 1, NTT - 1)])
                    for blk in range(11):
                        kp = 128 if blk < 10 else 96
                        v.wait_ge(P4W[blk % 2].sem, wtok[blk])
                        V.inc(v.tensor_scalar(
                            out=nl4[0:kp, :], in0=pk4[0:kp, blk % 2, :],
                            scalar1=15, scalar2=None, op0=ALU.bitwise_and))
                        V.inc(v.tensor_scalar(
                            out=nh4[0:kp, :], in0=pk4[0:kp, blk % 2, :],
                            scalar1=4, scalar2=None,
                            op0=ALU.logical_shift_right))
                        v.wait_ge(V.sem, V.v)
                        in1 = bass.AP(sr4, (blk % 2) * 512,
                                      [[1024, kp], [0, 4], [1, 512]])
                        V.inc(v.tensor_tensor(out=wd[0:kp, 0, blk],
                                              in0=nl4[0:kp, :], in1=in1,
                                              op=ALU.mult))
                        V.inc(v.tensor_tensor(out=wd[0:kp, 1, blk],
                                              in0=nh4[0:kp, :], in1=in1,
                                              op=ALU.mult))
                        v.wait_ge(V.sem, V.v)
                block.vector(p4_dq)
                assert V.v == vb4 + 4 * 11
                dq4_last[0] = V.v

                for tt in range(NTT):
                    def p4_lt(sy, d=d, tt=tt, ld4=ld4):
                        if tt + 1 < NTT:
                            if tt >= 1:
                                sy.wait_ge(TE.sem, mm4_tok[(d, tt - 1)])
                                sy.wait_ge(V.sem, add_tok[(d, tt - 1)])
                            ld4[tt + 1] = p4_hload(sy, d, tt + 1)
                    block.sync(p4_lt)

                    def p4_mm(te, d=d, tt=tt, bdtok=bdtok):
                        te.wait_ge(P4H[tt % 2].sem, ld4[tt])
                        if tt == 0:
                            te.wait_ge(BDC.sem, bdtok)
                        te.wait_ge(V.sem, dq4_last[0])
                        if add_tok.get((d, tt - 2)) is not None:
                            te.wait_ge(V.sem, add_tok[(d, tt - 2)])
                        elif d > 0 and tt < 2:
                            te.wait_ge(V.sem, add_tok[(d - 1, NTT - 2 + tt)])
                        par = tt % 2
                        for kt in range(88):
                            kp = 128 if kt < 80 else 96
                            te.matmul(od[par][:], hbuf[0:kp, par, kt, :],
                                      wd[0:kp, kt % 2, kt // 8,
                                         (kt % 8) // 2, :],
                                      start=(kt == 0), stop=False)
                        TE.inc(te.matmul(od[par][:],
                                         nrsh_t[:, tt * 128:(tt + 1) * 128],
                                         bd4[:], start=False, stop=True))
                    block.tensor(p4_mm)
                    mm4_tok[(d, tt)] = TE.v

                    def p4_xn(sc, d=d, tt=tt):
                        sc.wait_ge(P4H[tt % 2].sem, ld4[tt])
                        if add_tok.get((d, tt - 2)) is not None:
                            sc.wait_ge(V.sem, add_tok[(d, tt - 2)])
                        elif d > 0 and tt < 2:
                            sc.wait_ge(V.sem, add_tok[(d - 1, NTT - 2 + tt)])
                        S.inc(sc.activation(out=xnpc[:, tt % 2, :],
                                            in_=xpc[:, tt % 2, :],
                                            func=AF.Identity,
                                            scale=stats_b[:, tt, 1:2],
                                            bias=stats_b[:, tt, 2:3]))
                    block.scalar(p4_xn)
                    xnt4 = S.v

                    def p4_add(v, d=d, tt=tt, xnt4=xnt4):
                        par = tt % 2
                        v.wait_ge(TE.sem, mm4_tok[(d, tt)])
                        v.wait_ge(S.sem, xnt4)
                        if st_tok.get((d, tt - 2)) is not None:
                            v.wait_ge(ST4[tt % 2].sem, st_tok[(d, tt - 2)])
                        elif d > 0 and tt < 2:
                            v.wait_ge(ST4[tt % 2].sem,
                                      st_tok[(d - 1, NTT - 2 + tt)])
                        V.inc(v.tensor_tensor(out=osb[:, par, :],
                                              in0=od[par][:],
                                              in1=xpc[:, par, :], op=ALU.add))
                        v.wait_ge(V.sem, V.v)
                        V.inc(v.tensor_tensor(out=osb[:, par, :],
                                              in0=osb[:, par, :],
                                              in1=xnpc[:, par, :], op=ALU.add))
                        v.wait_ge(V.sem, V.v)
                        V.inc(v.tensor_tensor(out=osb[:, par, :],
                                              in0=osb[:, par, :],
                                              in1=msgpc[:, par, :],
                                              op=ALU.add))
                    block.vector(p4_add)
                    add_tok[(d, tt)] = V.v

                    def p4_st(sc, d=d, tt=tt):
                        sc.wait_ge(V.sem, add_tok[(d, tt)])
                        ST4[tt % 2].inc(sc.dma_start(
                            out=out_ap[tt * 128:(tt + 1) * 128,
                                       d * 512:(d + 1) * 512],
                            in_=osb[:, tt % 2, :]))
                    block.scalar(p4_st)
                    st_tok[(d, tt)] = ST4[tt % 2].v

            fins = [(ST4[0].sem, ST4[0].v), (ST4[1].sem, ST4[1].v),
                    (AQ.sem, AQ.v)]

            def p4_fin(sy, fins=fins):
                for sem, vv in fins:
                    sy.wait_ge(sem, vv)
            block.sync(p4_fin)

    return nc


_NC_CACHE = [None]


def kernel(L0_attn_x, L0_ffn_qweight_gu, L0_ffn_qzeros_gu, L0_ffn_scales_gu,
           L0_ffn_g_idx_gu, L0_ffn_qweight_down, L0_ffn_qzeros_down,
           L0_ffn_scales_down, L0_ffn_g_idx_down):
    x = np.ascontiguousarray(np.asarray(L0_attn_x, dtype=np.float32))
    qwg = np.ascontiguousarray(np.asarray(L0_ffn_qweight_gu, dtype=np.uint32))
    qzg = np.ascontiguousarray(np.asarray(L0_ffn_qzeros_gu, dtype=np.uint32))
    scg = np.ascontiguousarray(np.asarray(L0_ffn_scales_gu, dtype=np.float32))
    qwd = np.ascontiguousarray(np.asarray(L0_ffn_qweight_down, dtype=np.uint32))
    qzd = np.ascontiguousarray(np.asarray(L0_ffn_qzeros_down, dtype=np.uint32))
    scd = np.ascontiguousarray(np.asarray(L0_ffn_scales_down, dtype=np.float32))
    b8 = qwg.view(np.uint8).reshape(4, 128, 2, 86, 128, 4)
    qprep = np.ascontiguousarray(
        b8.transpose(3, 2, 1, 0, 5, 4)).reshape(86, 2, 128, 2048)
    gidx = 8 * np.arange(4)[None, :] + (np.arange(128) // 16)[:, None]
    sc3 = scg.reshape(32, 2, 86, 128)
    srprep = np.ascontiguousarray(sc3[gidx].transpose(3, 2, 0, 1, 4))
    qdp = np.zeros((11, 128, 4, 4096), np.uint8)
    qdp.reshape(1408, 4, 4096)[:1376] = (
        qwd.view(np.uint8).reshape(1376, 4096, 4).transpose(0, 2, 1))
    srd = np.ascontiguousarray(np.repeat(scd, 16, axis=0))

    if _NC_CACHE[0] is None:
        _NC_CACHE[0] = build()
    nc = _NC_CACHE[0]
    ims = [dict(x=x, x_tok=np.ascontiguousarray(x[c]), qprep=qprep,
                srprep=srprep, sc_gu=scg, qz_gu=qzg, qdprep=qdp, srep_d=srd,
                sc_d=scd, qz_d=qzd) for c in range(8)]
    res = run_bass_kernel_spmd(nc, ims, core_ids=list(range(8)))
    return np.stack([res.results[c]["out"] for c in range(8)], axis=0)


if __name__ == "__main__":
    import time
    t0 = time.time()
    nc = build()
    print("built", nc.n_instructions(), "instructions in",
          round(time.time() - t0, 1), "s")


# version 07
# speedup vs baseline: 1.0036x; 1.0036x over previous
import math
from contextlib import ExitStack

import numpy as np
import concourse.bass as bass
import concourse.mybir as mybir
from concourse.bass_utils import run_bass_kernel_spmd

F32 = mybir.dt.float32
BF16 = mybir.dt.bfloat16
U32 = mybir.dt.uint32
U8 = mybir.dt.uint8
AF = mybir.ActivationFunctionType
ALU = mybir.AluOpType
AX = mybir.AxisListType

B8 = 8
T = 2048
D = 4096
J = 22016
FF = 11008
NTT = 16
NTC = 4
NKT = 32
NA = 86
NCH = 32
HL = 88 * 128
HTT = 128 * HL
EPS = 1e-5
INVS = 1.0 / math.sqrt(2048.0)


class Ctr:
    def __init__(self, ctx, nc, name, step=1):
        self.sem = ctx.enter_context(nc.semaphore(name))
        self.step = step
        self.v = 0

    def inc(self, inst):
        self.v += self.step
        inst.then_inc(self.sem, self.step)
        return self.v


def build():
    nc = bass.Bass("TRN2", target_bir_lowering=False)

    x = nc.dram_tensor("x", [B8, T, D], F32, kind="ExternalInput")
    x_tok = nc.dram_tensor("x_tok", [T, D], F32, kind="ExternalInput")
    qprep = nc.dram_tensor("qprep", [NA, 2, 128, 2048], U8,
                           kind="ExternalInput")
    srprep = nc.dram_tensor("srprep", [NA, 2, 128, 4, 128], F32,
                            kind="ExternalInput")
    sc_gu = nc.dram_tensor("sc_gu", [32, J], F32, kind="ExternalInput")
    qz_gu = nc.dram_tensor("qz_gu", [32, J // 8], U32, kind="ExternalInput")
    qdprep = nc.dram_tensor("qdprep", [11, 128, 4, D], U8,
                            kind="ExternalInput")
    srep_d = nc.dram_tensor("srep_d", [1376, D], F32, kind="ExternalInput")
    sc_d = nc.dram_tensor("sc_d", [86, D], F32, kind="ExternalInput")
    qz_d = nc.dram_tensor("qz_d", [86, D // 8], U32, kind="ExternalInput")
    out = nc.dram_tensor("out", [T, D], F32, kind="ExternalOutput")

    h_dram = nc.dram_tensor("h_dram", [16, 128, 88, 128], BF16,
                            kind="Internal")
    msg_dram = nc.dram_tensor("msg_dram", [T, D], BF16, kind="Internal")
    bgu_dram = nc.dram_tensor("bgu_dram", [32, J], BF16, kind="Internal")
    bd_dram = nc.dram_tensor("bd_dram", [86, D], BF16, kind="Internal")

    xap = x.ap()
    xtk = x_tok.ap()
    out_ap = out.ap()
    bgu_ap = bgu_dram.ap()
    bd_ap = bd_dram.ap()
    msg_ap = msg_dram.ap()

    with ExitStack() as top:
        SQ = Ctr(top, nc, "SQ", 16)
        AQ = Ctr(top, nc, "AQ", 16)
        V = Ctr(top, nc, "V")
        S = Ctr(top, nc, "S")
        TE = Ctr(top, nc, "TE")
        G = Ctr(top, nc, "G")
        P0L = [Ctr(top, nc, "P0L%d" % i, 16) for i in range(4)]
        P2L = [Ctr(top, nc, "P2L%d" % i, 16) for i in range(2)]
        P3L = [Ctr(top, nc, "P3L%d" % i, 16) for i in range(2)]
        SP3 = [Ctr(top, nc, "SP3%d" % i, 16) for i in range(2)]
        P4W = [Ctr(top, nc, "P4W%d" % i, 16) for i in range(2)]
        BDC = Ctr(top, nc, "BDC", 16)
        P4H = [Ctr(top, nc, "P4H%d" % i, 16) for i in range(2)]
        ST4 = [Ctr(top, nc, "ST4%d" % i, 16) for i in range(2)]
        SCD = Ctr(top, nc, "SCD", 16)
        allq = ([SQ, AQ, V, S, TE, G] + P0L + P2L + P3L + SP3 + P4W +
                [BDC] + P4H + ST4 + [SCD])

        sb = lambda *a: top.enter_context(nc.sbuf_tensor(*a))
        ident = sb("ident", [128, 128], BF16)
        scratch_f = sb("scratch_f", [128, 172], F32)
        indneg = sb("indneg", [128, 172], BF16)
        ones1 = sb("ones1", [1, 128], BF16)
        stats_b = sb("stats_b", [128, NTT, 3], F32)
        stats_all = sb("stats_all", [128, B8, NTT, 3], F32)
        nrs_t = sb("nrs_t", [32, T], BF16)
        nrsh_t = sb("nrsh_t", [86, T], BF16)
        S_all = sb("S_all", [128, NCH, 8], F32)
        E_all = sb("E_all", [128, NCH, 8], BF16)

        block = top.enter_context(nc.Block())

        def barrier():
            snap = [(q.sem, q.v) for q in allq if q.v]
            for opener in (block.sync, block.vector, block.scalar,
                           block.tensor, block.gpsimd):
                def f(e, snap=snap):
                    for sem, v in snap:
                        e.wait_ge(sem, v)
                opener(f)

        def init_g(g):
            G.inc(g.iota(scratch_f[:, 0:128], [[-1, 128]], base=0,
                         channel_multiplier=1,
                         allow_small_or_imprecise_dtypes=True))
        block.gpsimd(init_g)
        g1 = G.v

        def init_v1(v):
            v.wait_ge(G.sem, g1)
            V.inc(v.tensor_scalar(out=ident[:], in0=scratch_f[:, 0:128],
                                  scalar1=0.0, scalar2=None, op0=ALU.is_equal))
            V.inc(v.memset(ones1[:], 1.0))
        block.vector(init_v1)
        v1 = V.v

        def init_g2(g):
            g.wait_ge(V.sem, v1)
            G.inc(g.iota(scratch_f[:], [[1, 172]], base=-86,
                         channel_multiplier=0,
                         allow_small_or_imprecise_dtypes=True))
        block.gpsimd(init_g2)
        g2 = G.v

        def init_v2(v):
            v.wait_ge(G.sem, g2)
            V.inc(v.tensor_scalar(out=indneg[:], in0=scratch_f[:], scalar1=0.0,
                                  scalar2=-1.0, op0=ALU.is_equal, op1=ALU.mult))
        block.vector(init_v2)

        with nc.sbuf_tensor("qzb", [32, J // 8], U32) as qzb, \
             nc.sbuf_tensor("zqt", [32, J // 4], U32) as zqt, \
             nc.sbuf_tensor("scb", [32, J // 4], F32) as scb, \
             nc.sbuf_tensor("bgu_x", [32, J // 4], BF16) as bgu_x, \
             nc.sbuf_tensor("qzdb", [86, D // 8], U32) as qzdb, \
             nc.sbuf_tensor("zqdt", [86, D], U32) as zqdt, \
             nc.sbuf_tensor("scdb", [86, D], F32) as scdb, \
             nc.sbuf_tensor("bd_x", [86, D], BF16) as bd_x:
            CJ = J // 4
            CP = J // 32

            def p1a_l0(sy):
                SQ.inc(sy.dma_start(out=qzb[:], in_=qz_gu.ap()))
                SQ.inc(sy.dma_start(out=qzdb[:], in_=qz_d.ap()))
            block.sync(p1a_l0)
            ld0 = SQ.v

            spill_prev = [0]
            vtok_prev = [0]
            for cc in range(4):
                def p1a_ls(sy, cc=cc):
                    if vtok_prev[0]:
                        sy.wait_ge(V.sem, vtok_prev[0])
                    SQ.inc(sy.dma_start(
                        out=scb[:], in_=sc_gu.ap()[:, cc * CJ:(cc + 1) * CJ]))
                block.sync(p1a_ls)
                scld = SQ.v

                def p1a_v(v, cc=cc, scld=scld):
                    v.wait_ge(SQ.sem, max(ld0, scld))
                    if spill_prev[0]:
                        v.wait_ge(AQ.sem, spill_prev[0])
                    for k in range(8):
                        V.inc(v.tensor_scalar(
                            out=bass.AP(zqt, k, [[CJ, 32], [8, CP]]),
                            in0=qzb[:, cc * CP:(cc + 1) * CP],
                            scalar1=4 * k, scalar2=15,
                            op0=ALU.logical_shift_right, op1=ALU.bitwise_and))
                    v.wait_ge(V.sem, V.v)
                    V.inc(v.tensor_scalar(out=zqt[:], in0=zqt[:], scalar1=1,
                                          scalar2=None, op0=ALU.add))
                    v.wait_ge(V.sem, V.v)
                    V.inc(v.tensor_tensor(out=bgu_x[:], in0=zqt[:], in1=scb[:],
                                          op=ALU.mult))
                block.vector(p1a_v)
                vtok_prev[0] = V.v
                vtok = V.v

                def p1a_s(sc, cc=cc, vtok=vtok):
                    sc.wait_ge(V.sem, vtok)
                    AQ.inc(sc.dma_start(out=bgu_ap[:, cc * CJ:(cc + 1) * CJ],
                                        in_=bgu_x[:]))
                block.scalar(p1a_s)
                spill_prev[0] = AQ.v

            def p1a_ld(sy):
                SCD.inc(sy.dma_start(out=scdb[:], in_=sc_d.ap()))
            block.sync(p1a_ld)
            scdld = SCD.v

            def p1a_vd(v, scdld=scdld):
                v.wait_ge(SCD.sem, scdld)
                for k in range(8):
                    V.inc(v.tensor_scalar(
                        out=bass.AP(zqdt, k, [[D, 86], [8, D // 8]]),
                        in0=qzdb[:], scalar1=4 * k, scalar2=15,
                        op0=ALU.logical_shift_right, op1=ALU.bitwise_and))
                v.wait_ge(V.sem, V.v)
                V.inc(v.tensor_scalar(out=zqdt[:], in0=zqdt[:], scalar1=1,
                                      scalar2=None, op0=ALU.add))
                v.wait_ge(V.sem, V.v)
                V.inc(v.tensor_tensor(out=bd_x[:], in0=zqdt[:], in1=scdb[:],
                                      op=ALU.mult))
            block.vector(p1a_vd)
            vdt = V.v

            def p1a_sd(sc, vdt=vdt):
                sc.wait_ge(V.sem, vdt)
                AQ.inc(sc.dma_start(out=bd_ap, in_=bd_x[:]))
            block.scalar(p1a_sd)

            barrier()

        with nc.sbuf_tensor("xnT", [128, NKT, T], BF16) as xnT:

            with nc.sbuf_tensor("xst", [128, D], F32) as xst, \
                 nc.sbuf_tensor("p1scrap", [128, D // 2], F32) as p1scrap, \
                 nc.sbuf_tensor("xnbf", [128, D], BF16) as xnbf, \
                 nc.sbuf_tensor("xnpm", [128, D], BF16) as xnpm, \
                 nc.sbuf_tensor("p1tmp", [128, 8], F32) as p1tmp, \
                 nc.sbuf_tensor("nrspf", [128, 32], F32) as nrspf, \
                 nc.sbuf_tensor("nrsp", [128, 32], BF16) as nrsp, \
                 nc.psum_tensor("pstr0", [128, 128], BF16) as pstr0, \
                 nc.psum_tensor("pstr1", [128, 128], BF16) as pstr1, \
                 nc.psum_tensor("psnrs", [32, 128], BF16) as psnrs:
                pstr = [pstr0, pstr1]
                prev = {"sxn": 0, "nrse": 0, "vvar": 0}
                for tt in range(NTT):
                    def p1_l(sy, tt=tt):
                        if prev["sxn"]:
                            sy.wait_ge(S.sem, prev["sxn"])
                        if prev["nrse"]:
                            sy.wait_ge(V.sem, prev["nrse"])
                        SQ.inc(sy.dma_start(
                            out=xst[:], in_=xtk[tt * 128:(tt + 1) * 128, :]))
                    block.sync(p1_l)
                    xld = SQ.v

                    def p1_v1(v, xld=xld):
                        v.wait_ge(SQ.sem, xld)
                        V.inc(v.reduce_sum(out=p1tmp[:, 0:1], in_=xst[:, :D // 2],
                                           axis=AX.X))
                        V.inc(v.reduce_sum(out=p1tmp[:, 1:2], in_=xst[:, D // 2:],
                                           axis=AX.X))
                    block.vector(p1_v1)
                    vsum = V.v

                    def p1_s1(sc, xld=xld):
                        sc.wait_ge(SQ.sem, xld)
                        if prev["vvar"]:
                            sc.wait_ge(V.sem, prev["vvar"])
                        S.inc(sc.activation(out=p1scrap[:], in_=xst[:, :D // 2],
                                            func=AF.Square,
                                            accum_out=p1tmp[:, 2:3]))
                        sc.wait_ge(S.sem, S.v)
                        S.inc(sc.activation(out=p1scrap[:], in_=xst[:, D // 2:],
                                            func=AF.Square,
                                            accum_out=p1tmp[:, 3:4]))
                    block.scalar(p1_s1)
                    ssq = S.v

                    def p1_v2(v, tt=tt, vsum=vsum, ssq=ssq):
                        v.wait_ge(V.sem, vsum)
                        v.wait_ge(S.sem, ssq)
                        V.inc(v.tensor_scalar(out=stats_b[:, tt, 0:1],
                                              in0=p1tmp[:, 0:1],
                                              scalar1=p1tmp[:, 1:2],
                                              scalar2=1.0 / D,
                                              op0=ALU.add, op1=ALU.mult))
                        V.inc(v.tensor_scalar(out=p1tmp[:, 5:6],
                                              in0=p1tmp[:, 2:3],
                                              scalar1=p1tmp[:, 3:4],
                                              scalar2=1.0 / D,
                                              op0=ALU.add, op1=ALU.mult))
                        v.wait_ge(V.sem, V.v)
                        V.inc(v.tensor_tensor(out=p1tmp[:, 6:7],
                                              in0=stats_b[:, tt, 0:1],
                                              in1=stats_b[:, tt, 0:1],
                                              op=ALU.mult))
                        v.wait_ge(V.sem, V.v)
                        V.inc(v.tensor_scalar(out=p1tmp[:, 5:6],
                                              in0=p1tmp[:, 5:6],
                                              scalar1=p1tmp[:, 6:7], scalar2=EPS,
                                              op0=ALU.subtract, op1=ALU.add))
                    block.vector(p1_v2)
                    vvar = V.v
                    prev["vvar"] = vvar

                    def p1_s2(sc, vvar=vvar):
                        sc.wait_ge(V.sem, vvar)
                        S.inc(sc.activation(out=p1tmp[:, 7:8], in_=p1tmp[:, 5:6],
                                            func=AF.Sqrt))
                    block.scalar(p1_s2)
                    ssd = S.v

                    def p1_v3(v, tt=tt, ssd=ssd):
                        v.wait_ge(S.sem, ssd)
                        V.inc(v.reciprocal(out=stats_b[:, tt, 1:2],
                                           in_=p1tmp[:, 7:8]))
                        v.wait_ge(V.sem, V.v)
                        V.inc(v.tensor_scalar(out=stats_b[:, tt, 2:3],
                                              in0=stats_b[:, tt, 0:1],
                                              scalar1=stats_b[:, tt, 1:2],
                                              scalar2=-1.0, op0=ALU.mult,
                                              op1=ALU.mult))
                    block.vector(p1_v3)
                    vst = V.v

                    def p1_s3(sc, tt=tt, vst=vst):
                        sc.wait_ge(V.sem, vst)
                        if prev["nrse"]:
                            sc.wait_ge(V.sem, prev["nrse"])
                        S.inc(sc.activation(out=xnbf[:], in_=xst[:],
                                            func=AF.Identity,
                                            scale=stats_b[:, tt, 1:2],
                                            bias=stats_b[:, tt, 2:3]))
                    block.scalar(p1_s3)
                    sxn = S.v
                    prev["sxn"] = sxn

                    def p1_v4(v, tt=tt, sxn=sxn):
                        v.wait_ge(S.sem, sxn)
                        if prev["nrse"]:
                            v.wait_ge(V.sem, prev["nrse"])
                        V.inc(v.tensor_copy(
                            out=xnpm[:],
                            in_=bass.AP(xnbf, 0,
                                        [[D, 128], [1024, 4], [1, 8], [8, 128]])))
                        V.inc(v.tensor_reduce(
                            out=nrspf[:],
                            in_=bass.AP(xnbf, 0, [[D, 128], [128, 32], [1, 128]]),
                            axis=AX.X, op=ALU.add, negate=True))
                        v.wait_ge(V.sem, V.v)
                        V.inc(v.tensor_copy(out=nrsp[:], in_=nrspf[:]))
                    block.vector(p1_v4)
                    vperm = V.v

                    evac_base = V.v

                    def p1_t(te, tt=tt, vperm=vperm, evac_base=evac_base):
                        te.wait_ge(V.sem, vperm)
                        for kt in range(NKT):
                            if kt >= 2:
                                te.wait_ge(V.sem, evac_base + kt - 1)
                            TE.inc(te.transpose(pstr[kt % 2][:],
                                                xnpm[:, kt * 128:(kt + 1) * 128],
                                                ident[:]))
                        TE.inc(te.transpose(psnrs[:], nrsp[:], ident[:]))
                    tr_base = TE.v
                    block.tensor(p1_t)

                    def p1_v5(v, tt=tt, tr_base=tr_base):
                        for kt in range(NKT):
                            v.wait_ge(TE.sem, tr_base + kt + 1)
                            V.inc(v.tensor_copy(
                                out=xnT[:, kt, tt * 128:(tt + 1) * 128],
                                in_=pstr[kt % 2][:]))
                        v.wait_ge(TE.sem, tr_base + NKT + 1)
                        V.inc(v.tensor_copy(out=nrs_t[:, tt * 128:(tt + 1) * 128],
                                            in_=psnrs[:]))
                    block.vector(p1_v5)
                    assert V.v == evac_base + NKT + 1
                    prev["nrse"] = V.v

            barrier()

            with nc.sbuf_tensor("xst0", [128, 4, D // 2], F32) as xst0, \
                 nc.sbuf_tensor("p0scrap", [128, D // 2], F32) as p0scrap, \
                 nc.sbuf_tensor("p0sum", [128, NTT, 2], F32) as p0sum, \
                 nc.sbuf_tensor("p0sq", [128, NTT, 2], F32) as p0sq, \
                 nc.sbuf_tensor("p0t", [128, NTT, 4], F32) as p0t:
                prev_v3 = [0]
                for m in range(B8):
                    vb = V.v
                    sbase = S.v
                    p0toks = []

                    def p0_l(sy, m=m, vb=vb, sbase=sbase, p0toks=p0toks):
                        for i in range(32):
                            if i >= 4:
                                sy.wait_ge(V.sem, vb + i - 3)
                                sy.wait_ge(S.sem, sbase + i - 3)
                            elif m > 0:
                                sy.wait_ge(V.sem, vb)
                                sy.wait_ge(S.sem, sbase)
                            tt, hh = divmod(i, 2)
                            p0toks.append(P0L[i % 4].inc(sy.dma_start(
                                out=xst0[:, i % 4, :],
                                in_=xap[m, tt * 128:(tt + 1) * 128,
                                        hh * (D // 2):(hh + 1) * (D // 2)])))
                    block.sync(p0_l)

                    def p0_v1(v, m=m, p0toks=p0toks):
                        if prev_v3[0]:
                            v.wait_ge(V.sem, prev_v3[0])
                        for i in range(32):
                            tt, hh = divmod(i, 2)
                            v.wait_ge(P0L[i % 4].sem, p0toks[i])
                            V.inc(v.reduce_sum(out=p0sum[:, tt, hh:hh + 1],
                                               in_=xst0[:, i % 4, :], axis=AX.X))
                    block.vector(p0_v1)
                    assert V.v == vb + 32

                    def p0_s1(sc, m=m, p0toks=p0toks):
                        if prev_v3[0]:
                            sc.wait_ge(V.sem, prev_v3[0])
                        for i in range(32):
                            tt, hh = divmod(i, 2)
                            sc.wait_ge(P0L[i % 4].sem, p0toks[i])
                            S.inc(sc.activation(out=p0scrap[:],
                                                in_=xst0[:, i % 4, :],
                                                func=AF.Square,
                                                accum_out=p0sq[:, tt, hh:hh + 1]))
                    block.scalar(p0_s1)
                    assert S.v == sbase + 32
                    s1 = S.v
                    v1b = V.v

                    def p0_v2(v, m=m, s1=s1, v1b=v1b):
                        v.wait_ge(V.sem, v1b)
                        v.wait_ge(S.sem, s1)
                        for tt in range(NTT):
                            V.inc(v.tensor_scalar(
                                out=stats_all[:, m, tt, 0:1],
                                in0=p0sum[:, tt, 0:1], scalar1=p0sum[:, tt, 1:2],
                                scalar2=1.0 / D, op0=ALU.add, op1=ALU.mult))
                            V.inc(v.tensor_scalar(
                                out=p0t[:, tt, 0:1],
                                in0=p0sq[:, tt, 0:1], scalar1=p0sq[:, tt, 1:2],
                                scalar2=1.0 / D, op0=ALU.add, op1=ALU.mult))
                        v.wait_ge(V.sem, V.v)
                        for tt in range(NTT):
                            V.inc(v.tensor_tensor(out=p0t[:, tt, 1:2],
                                                  in0=stats_all[:, m, tt, 0:1],
                                                  in1=stats_all[:, m, tt, 0:1],
                                                  op=ALU.mult))
                        v.wait_ge(V.sem, V.v)
                        for tt in range(NTT):
                            V.inc(v.tensor_scalar(out=p0t[:, tt, 2:3],
                                                  in0=p0t[:, tt, 0:1],
                                                  scalar1=p0t[:, tt, 1:2],
                                                  scalar2=EPS, op0=ALU.subtract,
                                                  op1=ALU.add))
                    block.vector(p0_v2)
                    v2 = V.v

                    def p0_s2(sc, m=m, v2=v2):
                        sc.wait_ge(V.sem, v2)
                        for tt in range(NTT):
                            S.inc(sc.activation(out=p0t[:, tt, 3:4],
                                                in_=p0t[:, tt, 2:3],
                                                func=AF.Sqrt))
                    block.scalar(p0_s2)
                    s2 = S.v

                    def p0_v3(v, m=m, s2=s2):
                        v.wait_ge(S.sem, s2)
                        for tt in range(NTT):
                            V.inc(v.reciprocal(out=stats_all[:, m, tt, 1:2],
                                               in_=p0t[:, tt, 3:4]))
                        v.wait_ge(V.sem, V.v)
                        for tt in range(NTT):
                            V.inc(v.tensor_scalar(out=stats_all[:, m, tt, 2:3],
                                                  in0=stats_all[:, m, tt, 0:1],
                                                  scalar1=stats_all[:, m, tt, 1:2],
                                                  scalar2=-1.0, op0=ALU.mult,
                                                  op1=ALU.mult))
                    block.vector(p0_v3)
                    prev_v3[0] = V.v

            barrier()

            with nc.sbuf_tensor("xq", [128, B8, NTT, 128], BF16) as xq, \
                 nc.sbuf_tensor("xqb", [128, NTT, 128], BF16) as xqb, \
                 nc.sbuf_tensor("xst2", [128, 2, NTT, 128], F32) as xst2, \
                 nc.sbuf_tensor("macc", [128, NTT, 128], BF16) as macc, \
                 nc.sbuf_tensor("mprod", [128, NTT, 128], BF16) as mprod, \
                 nc.sbuf_tensor("msk", [128, 128], BF16) as msk, \
                 nc.sbuf_tensor("et_sb", [1, 1024], BF16) as et_sb, \
                 nc.sbuf_tensor("erep", [128, 1024], BF16) as erep, \
                 nc.sbuf_tensor("p2t", [128, 4], F32) as p2t, \
                 nc.sbuf_tensor("e_f", [128, 8], F32) as e_f, \
                 nc.psum_tensor("gps0", [128, 128], F32) as gps0, \
                 nc.psum_tensor("gps1", [128, 128], F32) as gps1, \
                 nc.psum_tensor("etps", [1, 1024], BF16) as etps, \
                 nc.psum_tensor("erps0", [128, 512], F32) as erps0, \
                 nc.psum_tensor("erps1", [128, 512], F32) as erps1:
                gps = [gps0, gps1]
                erps = [erps0, erps1]
                pc = {"gram_last": 0, "diag_last": 0, "spill": 0,
                      "msg_last": 0, "ete": 0, "erepe": 0, "norm_last": 0,
                      "exp": 0}
                for ch in range(NCH):
                    norm_toks = []
                    for m in range(B8 + 1):
                        def p2_l(sy, ch=ch, m=m):
                            if m >= 2:
                                sy.wait_ge(V.sem, norm_toks[m - 2])
                            elif pc["norm_last"]:
                                sy.wait_ge(V.sem, pc["norm_last"])
                            if m < B8:
                                src = xap[m].rearrange("(ct p) l -> p ct l",
                                                       p=128)[:, :,
                                                              ch * 128:(ch + 1) * 128]
                                P2L[m % 2].inc(sy.dma_start(out=xst2[:, m % 2],
                                                            in_=src))
                            else:
                                src = xtk.rearrange("(ct p) l -> p ct l",
                                                    p=128)[:, :,
                                                           ch * 128:(ch + 1) * 128]
                                P2L[m % 2].inc(sy.dma_start(out=xst2[:, m % 2],
                                                            in_=src))
                        block.sync(p2_l)
                        lt = P2L[m % 2].v

                        def p2_n(v, ch=ch, m=m, lt=lt):
                            v.wait_ge(P2L[m % 2].sem, lt)
                            if m == 0 and pc["gram_last"]:
                                v.wait_ge(TE.sem, pc["gram_last"])
                            st = stats_all[:, m] if m < B8 else stats_b
                            for ct in range(NTT):
                                dst = xq[:, m, ct, :] if m < B8 else xqb[:, ct, :]
                                V.inc(v.tensor_scalar(
                                    out=dst, in0=xst2[:, m % 2, ct, :],
                                    scalar1=st[:, ct, 1:2],
                                    scalar2=st[:, ct, 2:3],
                                    op0=ALU.mult, op1=ALU.add))
                        block.vector(p2_n)
                        norm_toks.append(V.v)
                    pc["norm_last"] = V.v

                    diag_pred = V.v

                    def p2_g(te, ch=ch, diag_pred=diag_pred):
                        te.wait_ge(V.sem, pc["norm_last"])
                        for m in range(B8):
                            if m >= 2:
                                te.wait_ge(V.sem, diag_pred + 2 * (m - 1))
                            elif pc["diag_last"]:
                                te.wait_ge(V.sem, pc["diag_last"])
                            for ct in range(NTT):
                                mm = te.matmul(gps[m % 2][:], xqb[:, ct, :],
                                               xq[:, m, ct, :],
                                               start=(ct == 0),
                                               stop=(ct == NTT - 1))
                                if ct == NTT - 1:
                                    TE.inc(mm)
                    gram_base = TE.v
                    block.tensor(p2_g)
                    pc["gram_last"] = TE.v

                    def p2_d(v, ch=ch, gram_base=gram_base):
                        if pc["exp"]:
                            v.wait_ge(S.sem, pc["exp"])
                        for m in range(B8):
                            v.wait_ge(TE.sem, gram_base + m + 1)
                            V.inc(v.tensor_tensor(out=msk[:], in0=gps[m % 2][:],
                                                  in1=ident[:], op=ALU.mult))
                            v.wait_ge(V.sem, V.v)
                            V.inc(v.reduce_sum(out=S_all[:, ch, m:m + 1],
                                               in_=msk[:], axis=AX.X))
                        v.wait_ge(V.sem, V.v)
                        V.inc(v.reduce_max(out=p2t[:, 0:1], in_=S_all[:, ch, :],
                                           axis=AX.X))
                        v.wait_ge(V.sem, V.v)
                        V.inc(v.tensor_scalar(out=p2t[:, 1:2], in0=p2t[:, 0:1],
                                              scalar1=-INVS, scalar2=None,
                                              op0=ALU.mult))
                    block.vector(p2_d)
                    assert V.v == diag_pred + 2 * B8 + 2
                    pc["diag_last"] = V.v
                    vdiag = V.v

                    def p2_e(sc, ch=ch, vdiag=vdiag):
                        sc.wait_ge(V.sem, vdiag)
                        S.inc(sc.activation(out=e_f[:], in_=S_all[:, ch, :],
                                            func=AF.Exp, scale=INVS,
                                            bias=p2t[:, 1:2]))
                    block.scalar(p2_e)
                    pc["exp"] = S.v
                    sexp = S.v

                    def p2_sm(v, ch=ch, sexp=sexp):
                        v.wait_ge(S.sem, sexp)
                        V.inc(v.reduce_sum(out=p2t[:, 2:3], in_=e_f[:],
                                           axis=AX.X))
                        v.wait_ge(V.sem, V.v)
                        V.inc(v.reciprocal(out=p2t[:, 3:4], in_=p2t[:, 2:3]))
                        v.wait_ge(V.sem, V.v)
                        V.inc(v.tensor_scalar(out=E_all[:, ch, :], in0=e_f[:],
                                              scalar1=p2t[:, 3:4], scalar2=None,
                                              op0=ALU.mult))
                    block.vector(p2_sm)
                    ven = V.v

                    def p2_t1(te, ch=ch, ven=ven):
                        te.wait_ge(V.sem, ven)
                        if pc["ete"]:
                            te.wait_ge(V.sem, pc["ete"])
                        for m in range(B8):
                            TE.inc(te.transpose(etps[0:1, m * 128:(m + 1) * 128],
                                                E_all[:, ch, m:m + 1], ident[:]))
                    block.tensor(p2_t1)
                    ttr = TE.v

                    def p2_ev(v, ch=ch, ttr=ttr):
                        v.wait_ge(TE.sem, ttr)
                        V.inc(v.tensor_copy(out=et_sb[:], in_=etps[:]))
                    block.vector(p2_ev)
                    pc["ete"] = V.v
                    vet = V.v

                    def p2_t2(te, ch=ch, vet=vet):
                        te.wait_ge(V.sem, vet)
                        if pc["erepe"]:
                            te.wait_ge(V.sem, pc["erepe"])
                        for m in range(B8):
                            TE.inc(te.matmul(
                                erps[m // 4][:, (m % 4) * 128:(m % 4 + 1) * 128],
                                ones1[:], et_sb[0:1, m * 128:(m + 1) * 128],
                                start=True, stop=True))
                    block.tensor(p2_t2)
                    terep = TE.v

                    def p2_m(v, ch=ch, terep=terep):
                        v.wait_ge(TE.sem, terep)
                        if pc["spill"]:
                            v.wait_ge(AQ.sem, pc["spill"])
                        V.inc(v.tensor_copy(out=erep[:, 0:512], in_=erps[0][:]))
                        V.inc(v.tensor_copy(out=erep[:, 512:1024],
                                            in_=erps[1][:]))
                        v.wait_ge(V.sem, V.v)
                        pc["erepe"] = V.v
                        for m in range(B8):
                            bcast = bass.AP(erep, m * 128,
                                            [[1024, 128], [0, NTT], [1, 128]])
                            if m == 0:
                                V.inc(v.tensor_tensor(out=macc[:], in0=xq[:, 0],
                                                      in1=bcast, op=ALU.mult))
                            else:
                                V.inc(v.tensor_tensor(out=mprod[:],
                                                      in0=xq[:, m], in1=bcast,
                                                      op=ALU.mult))
                                v.wait_ge(V.sem, V.v)
                                V.inc(v.tensor_tensor(out=macc[:], in0=macc[:],
                                                      in1=mprod[:], op=ALU.add))
                            v.wait_ge(V.sem, V.v)
                    block.vector(p2_m)
                    pc["msg_last"] = V.v
                    vmsg = V.v

                    def p2_sp(sc, ch=ch, vmsg=vmsg):
                        sc.wait_ge(V.sem, vmsg)
                        dst = bass.AP(msg_dram, ch * 128,
                                      [[D, 128], [128 * D, NTT], [1, 128]])
                        AQ.inc(sc.dma_start(out=dst, in_=macc[:]))
                    block.scalar(p2_sp)
                    pc["spill"] = AQ.v

            barrier()

            with nc.sbuf_tensor("wring", [128, 2, 2, 2, 4, 4, 128], BF16) as wring, \
                 nc.sbuf_tensor("pkr", [128, 2, 2, 2048], U8) as pkr, \
                 nc.sbuf_tensor("srr", [128, 2, 2, 4, 128], F32) as srr, \
                 nc.sbuf_tensor("bgur", [32, 2, 2, 128], BF16) as bgur, \
                 nc.sbuf_tensor("nl3", [128, 2048], U8) as nl3, \
                 nc.sbuf_tensor("nh3", [128, 2048], U8) as nh3, \
                 nc.sbuf_tensor("hstage", [128, 8, 512], BF16) as hstage, \
                 nc.sbuf_tensor("sig", [128, 2, 512], F32) as sig, \
                 nc.psum_tensor("pg0", [128, 512], F32) as pg0, \
                 nc.psum_tensor("pg1", [128, 512], F32) as pg1, \
                 nc.psum_tensor("pu0", [128, 512], F32) as pu0, \
                 nc.psum_tensor("pu1", [128, 512], F32) as pu1, \
                 nc.psum_tensor("rs0", [86, 512], F32) as rs0, \
                 nc.psum_tensor("rs1", [86, 512], F32) as rs1, \
                 nc.psum_tensor("rs2", [86, 512], F32) as rs2, \
                 nc.psum_tensor("rs3", [86, 512], F32) as rs3:
                pg = [pg0, pg1]
                pu = [pu0, pu1]
                rsp = [rs0, rs1, rs2, rs3]

                def p3_loads(sy, a, war_tok, te_tok):
                    if war_tok:
                        sy.wait_ge(V.sem, war_tok)
                    if te_tok:
                        sy.wait_ge(TE.sem, te_tok)
                    sl = a % 2
                    qp = qprep.ap()
                    sp = srprep.ap()
                    for s, base in ((0, 0), (1, FF)):
                        off = base + a * 128
                        P3L[sl].inc(sy.dma_start(out=pkr[:, sl, s],
                                                 in_=qp[a, s]))
                        P3L[sl].inc(sy.dma_start(out=srr[:, sl, s],
                                                 in_=sp[a, s]))
                        P3L[sl].inc(sy.dma_start(out=bgur[:, sl, s],
                                                 in_=bgu_ap[:, off:off + 128]))

                block.sync(lambda sy: p3_loads(sy, 0, 0, 0))
                ld_tok = {0: P3L[0].v}
                deq_tok = {}
                hT_tok = {}
                silu_tok = {}
                spill_tok3 = {}
                mm_tok = {}

                for a in range(NA):
                    def p3_l(sy, a=a):
                        if a + 1 < NA:
                            p3_loads(sy, a + 1, deq_tok.get(a - 1, 0),
                                     mm_tok.get(a - 1, 0))
                    block.sync(p3_l)
                    if a + 1 < NA:
                        ld_tok[a + 1] = P3L[(a + 1) % 2].v

                    def p3_dq(v, a=a):
                        v.wait_ge(P3L[a % 2].sem, ld_tok[a])
                        if a >= 2:
                            v.wait_ge(TE.sem, mm_tok[a - 2])
                        sl = a % 2
                        for s in range(2):
                            V.inc(v.tensor_scalar(
                                out=nl3[:], in0=pkr[:, sl, s, :],
                                scalar1=15, scalar2=None, op0=ALU.bitwise_and))
                            V.inc(v.tensor_scalar(
                                out=nh3[:], in0=pkr[:, sl, s, :],
                                scalar1=4, scalar2=None,
                                op0=ALU.logical_shift_right))
                            v.wait_ge(V.sem, V.v)
                            in1 = bass.AP(srr, (sl * 2 + s) * 512,
                                          [[2048, 128], [128, 4],
                                           [0, 4], [1, 128]])
                            V.inc(v.tensor_tensor(
                                out=wring[:, sl, s, 0], in0=nl3[:],
                                in1=in1, op=ALU.mult))
                            V.inc(v.tensor_tensor(
                                out=wring[:, sl, s, 1], in0=nh3[:],
                                in1=in1, op=ALU.mult))
                            v.wait_ge(V.sem, V.v)
                    block.vector(p3_dq)
                    deq_tok[a] = V.v
                    vpred = V.v
                    silu_pred = S.v
                    mmb = TE.v

                    def p3_mm(te, a=a, vpred=vpred, silu_pred=silu_pred):
                        te.wait_ge(V.sem, deq_tok[a])
                        sl = a % 2
                        for tc in range(NTC):
                            if tc >= 2:
                                te.wait_ge(S.sem, silu_pred + tc - 1)
                                te.wait_ge(V.sem, vpred + tc - 1)
                            elif a >= 1:
                                te.wait_ge(S.sem, silu_tok[a - 1])
                                te.wait_ge(V.sem, hT_tok[a - 1])
                            for kt in range(NKT):
                                for s, bank in ((0, pg[tc % 2]),
                                                (1, pu[tc % 2])):
                                    wsl = wring[:, sl, s, kt % 2, kt // 8,
                                                (kt % 8) // 2, :]
                                    te.matmul(bank[:], wsl,
                                              xnT[:, kt, tc * 512:(tc + 1) * 512],
                                              start=(kt == 0), stop=False)
                            for s, bank in ((0, pg[tc % 2]), (1, pu[tc % 2])):
                                TE.inc(te.matmul(
                                    bank[:], bgur[:, sl, s],
                                    nrs_t[:, tc * 512:(tc + 1) * 512],
                                    start=False, stop=True))
                        if a >= 1:
                            for tc in range(NTC):
                                slot = ((a - 1) * NTC + tc) % 8
                                TE.inc(te.matmul(
                                    rsp[tc][:],
                                    indneg[:, 86 - (a - 1):172 - (a - 1)],
                                    hstage[:, slot, :],
                                    start=(a - 1 == 0), stop=False))
                    block.tensor(p3_mm)
                    mm_tok[a] = TE.v

                    def p3_si(sc, a=a, mmb=mmb, vpred=vpred):
                        for tc in range(NTC):
                            sc.wait_ge(TE.sem, mmb + 2 * tc + 1)
                            if tc >= 2:
                                sc.wait_ge(V.sem, vpred + tc - 1)
                            elif a >= 1:
                                sc.wait_ge(V.sem, hT_tok[a - 1])
                            S.inc(sc.activation(out=sig[:, tc % 2, :],
                                                in_=pg[tc % 2][:],
                                                func=AF.Silu))
                    block.scalar(p3_si)
                    assert S.v == silu_pred + NTC
                    silu_tok[a] = S.v

                    def p3_h(v, a=a, mmb=mmb, silu_pred=silu_pred):
                        for tc in range(NTC):
                            slot = (a * NTC + tc) % 8
                            v.wait_ge(S.sem, silu_pred + tc + 1)
                            v.wait_ge(TE.sem, mmb + 2 * tc + 2)
                            if spill_tok3.get(a - 2) is not None:
                                v.wait_ge(SP3[a % 2].sem, spill_tok3[a - 2])
                            V.inc(v.tensor_tensor(out=hstage[:, slot, :],
                                                  in0=sig[:, tc % 2, :],
                                                  in1=pu[tc % 2][:],
                                                  op=ALU.mult))
                    block.vector(p3_h)
                    assert V.v == vpred + NTC
                    hT_tok[a] = V.v
                    hbase = V.v - NTC

                    def p3_sp(sc, a=a, hbase=hbase):
                        for tc in range(NTC):
                            slot = (a * NTC + tc) % 8
                            sc.wait_ge(V.sem, hbase + tc + 1)
                            if a < 80:
                                off = (16 * (a % 8)) * HL + (8 * (a // 8)) * 128
                            else:
                                off = (16 * (a - 80)) * HL + 80 * 128
                            for c in range(4):
                                dst = bass.AP(h_dram,
                                              (tc * 4 + c) * HTT + off,
                                              [[HL, 16], [128, 8], [1, 128]])
                                SP3[a % 2].inc(sc.dma_start(
                                    out=dst,
                                    in_=hstage[:, slot,
                                               c * 128:(c + 1) * 128]))
                        spill_tok3[a] = SP3[a % 2].v
                    block.scalar(p3_sp)

                def p3_ep_t(te):
                    a = NA - 1
                    te.wait_ge(V.sem, hT_tok[a])
                    for tc in range(NTC):
                        slot = (a * NTC + tc) % 8
                        TE.inc(te.matmul(rsp[tc][:], indneg[:, 86 - a:172 - a],
                                         hstage[:, slot, :], start=False,
                                         stop=True))
                block.tensor(p3_ep_t)
                rs_done = TE.v

                def p3_ep_v(v, rs_done=rs_done):
                    v.wait_ge(TE.sem, rs_done)
                    for tc in range(NTC):
                        V.inc(v.tensor_copy(
                            out=nrsh_t[:, tc * 512:(tc + 1) * 512],
                            in_=rsp[tc][:]))
                block.vector(p3_ep_v)

                barrier()

        with nc.sbuf_tensor("wd", [128, 2, 11, 4, 512], BF16) as wd, \
             nc.sbuf_tensor("pk4", [128, 2, 2048], U8) as pk4, \
             nc.sbuf_tensor("sr4", [128, 2, 512], F32) as sr4, \
             nc.sbuf_tensor("nl4", [128, 2048], U8) as nl4, \
             nc.sbuf_tensor("nh4", [128, 2048], U8) as nh4, \
             nc.sbuf_tensor("bd4", [86, 512], BF16) as bd4, \
             nc.sbuf_tensor("hbuf", [128, 2, 88, 128], BF16) as hbuf, \
             nc.sbuf_tensor("xpc", [128, 2, 512], F32) as xpc, \
             nc.sbuf_tensor("xnpc", [128, 2, 512], F32) as xnpc, \
             nc.sbuf_tensor("msgpc", [128, 2, 512], BF16) as msgpc, \
             nc.sbuf_tensor("osb", [128, 2, 512], F32) as osb, \
             nc.psum_tensor("od0", [128, 512], F32) as od0, \
             nc.psum_tensor("od1", [128, 512], F32) as od1:
            od = [od0, od1]

            def p4_hload(sy, d, tt):
                par = tt % 2
                P4H[par].inc(sy.dma_start(
                    out=hbuf[:, par, :, :],
                    in_=bass.AP(h_dram, tt * HTT,
                                [[HL, 128], [128, 88], [1, 128]])))
                P4H[par].inc(sy.dma_start(
                    out=xpc[:, par, :],
                    in_=xtk[tt * 128:(tt + 1) * 128, d * 512:(d + 1) * 512]))
                P4H[par].inc(sy.dma_start(
                    out=msgpc[:, par, :],
                    in_=msg_ap[tt * 128:(tt + 1) * 128,
                               d * 512:(d + 1) * 512]))
                return P4H[par].v

            mm4_tok = {}
            add_tok = {}
            st_tok = {}
            dq4_last = [0]
            for d in range(8):
                vb4 = V.v
                wtok = {}
                ld4 = {}

                def p4_l(sy, d=d, vb4=vb4, wtok=wtok, ld4=ld4):
                    if d > 0:
                        sy.wait_ge(TE.sem, mm4_tok[(d - 1, NTT - 1)])
                        sy.wait_ge(V.sem, add_tok[(d - 1, NTT - 1)])
                    BDC.inc(sy.dma_start(out=bd4[:],
                                         in_=bd_ap[:, d * 512:(d + 1) * 512]))
                    for blk in range(11):
                        if blk >= 2:
                            sy.wait_ge(V.sem, vb4 + 4 * (blk - 1))
                        kp = 128 if blk < 10 else 96
                        P4W[blk % 2].inc(sy.dma_start(
                            out=pk4[0:kp, blk % 2, :],
                            in_=qdprep.ap()[blk, 0:kp, :,
                                            d * 512:(d + 1) * 512]))
                        wtok[blk] = P4W[blk % 2].inc(sy.dma_start(
                            out=sr4[0:kp, blk % 2, :],
                            in_=bass.AP(srep_d, (128 * blk) * D + d * 512,
                                        [[D, kp], [1, 512]])))
                    ld4[0] = p4_hload(sy, d, 0)
                block.sync(p4_l)
                bdtok = BDC.v

                def p4_dq(v, d=d, wtok=wtok):
                    if d > 0:
                        v.wait_ge(TE.sem, mm4_tok[(d - 1, NTT - 1)])
                    for blk in range(11):
                        kp = 128 if blk < 10 else 96
                        v.wait_ge(P4W[blk % 2].sem, wtok[blk])
                        V.inc(v.tensor_scalar(
                            out=nl4[0:kp, :], in0=pk4[0:kp, blk % 2, :],
                            scalar1=15, scalar2=None, op0=ALU.bitwise_and))
                        V.inc(v.tensor_scalar(
                            out=nh4[0:kp, :], in0=pk4[0:kp, blk % 2, :],
                            scalar1=4, scalar2=None,
                            op0=ALU.logical_shift_right))
                        v.wait_ge(V.sem, V.v)
                        in1 = bass.AP(sr4, (blk % 2) * 512,
                                      [[1024, kp], [0, 4], [1, 512]])
                        V.inc(v.tensor_tensor(out=wd[0:kp, 0, blk],
                                              in0=nl4[0:kp, :], in1=in1,
                                              op=ALU.mult))
                        V.inc(v.tensor_tensor(out=wd[0:kp, 1, blk],
                                              in0=nh4[0:kp, :], in1=in1,
                                              op=ALU.mult))
                        v.wait_ge(V.sem, V.v)
                block.vector(p4_dq)
                assert V.v == vb4 + 4 * 11
                dq4_last[0] = V.v

                for tt in range(NTT):
                    def p4_lt(sy, d=d, tt=tt, ld4=ld4):
                        if tt + 1 < NTT:
                            if tt >= 1:
                                sy.wait_ge(TE.sem, mm4_tok[(d, tt - 1)])
                                sy.wait_ge(V.sem, add_tok[(d, tt - 1)])
                            ld4[tt + 1] = p4_hload(sy, d, tt + 1)
                    block.sync(p4_lt)

                    def p4_mm(te, d=d, tt=tt, bdtok=bdtok):
                        te.wait_ge(P4H[tt % 2].sem, ld4[tt])
                        if tt == 0:
                            te.wait_ge(BDC.sem, bdtok)
                        te.wait_ge(V.sem, dq4_last[0])
                        if add_tok.get((d, tt - 2)) is not None:
                            te.wait_ge(V.sem, add_tok[(d, tt - 2)])
                        elif d > 0 and tt < 2:
                            te.wait_ge(V.sem, add_tok[(d - 1, NTT - 2 + tt)])
                        par = tt % 2
                        for kt in range(88):
                            kp = 128 if kt < 80 else 96
                            te.matmul(od[par][:], hbuf[0:kp, par, kt, :],
                                      wd[0:kp, kt % 2, kt // 8,
                                         (kt % 8) // 2, :],
                                      start=(kt == 0), stop=False)
                        TE.inc(te.matmul(od[par][:],
                                         nrsh_t[:, tt * 128:(tt + 1) * 128],
                                         bd4[:], start=False, stop=True))
                    block.tensor(p4_mm)
                    mm4_tok[(d, tt)] = TE.v

                    def p4_xn(sc, d=d, tt=tt):
                        sc.wait_ge(P4H[tt % 2].sem, ld4[tt])
                        if add_tok.get((d, tt - 2)) is not None:
                            sc.wait_ge(V.sem, add_tok[(d, tt - 2)])
                        elif d > 0 and tt < 2:
                            sc.wait_ge(V.sem, add_tok[(d - 1, NTT - 2 + tt)])
                        S.inc(sc.activation(out=xnpc[:, tt % 2, :],
                                            in_=xpc[:, tt % 2, :],
                                            func=AF.Identity,
                                            scale=stats_b[:, tt, 1:2],
                                            bias=stats_b[:, tt, 2:3]))
                    block.scalar(p4_xn)
                    xnt4 = S.v

                    def p4_add(v, d=d, tt=tt, xnt4=xnt4):
                        par = tt % 2
                        v.wait_ge(TE.sem, mm4_tok[(d, tt)])
                        v.wait_ge(S.sem, xnt4)
                        if st_tok.get((d, tt - 2)) is not None:
                            v.wait_ge(ST4[tt % 2].sem, st_tok[(d, tt - 2)])
                        elif d > 0 and tt < 2:
                            v.wait_ge(ST4[tt % 2].sem,
                                      st_tok[(d - 1, NTT - 2 + tt)])
                        V.inc(v.tensor_tensor(out=osb[:, par, :],
                                              in0=od[par][:],
                                              in1=xpc[:, par, :], op=ALU.add))
                        v.wait_ge(V.sem, V.v)
                        V.inc(v.tensor_tensor(out=osb[:, par, :],
                                              in0=osb[:, par, :],
                                              in1=xnpc[:, par, :], op=ALU.add))
                        v.wait_ge(V.sem, V.v)
                        V.inc(v.tensor_tensor(out=osb[:, par, :],
                                              in0=osb[:, par, :],
                                              in1=msgpc[:, par, :],
                                              op=ALU.add))
                    block.vector(p4_add)
                    add_tok[(d, tt)] = V.v

                    def p4_st(sc, d=d, tt=tt):
                        sc.wait_ge(V.sem, add_tok[(d, tt)])
                        ST4[tt % 2].inc(sc.dma_start(
                            out=out_ap[tt * 128:(tt + 1) * 128,
                                       d * 512:(d + 1) * 512],
                            in_=osb[:, tt % 2, :]))
                    block.scalar(p4_st)
                    st_tok[(d, tt)] = ST4[tt % 2].v

            fins = [(ST4[0].sem, ST4[0].v), (ST4[1].sem, ST4[1].v),
                    (AQ.sem, AQ.v)]

            def p4_fin(sy, fins=fins):
                for sem, vv in fins:
                    sy.wait_ge(sem, vv)
            block.sync(p4_fin)

    return nc


_NC_CACHE = [None]


def kernel(L0_attn_x, L0_ffn_qweight_gu, L0_ffn_qzeros_gu, L0_ffn_scales_gu,
           L0_ffn_g_idx_gu, L0_ffn_qweight_down, L0_ffn_qzeros_down,
           L0_ffn_scales_down, L0_ffn_g_idx_down):
    x = np.ascontiguousarray(np.asarray(L0_attn_x, dtype=np.float32))
    qwg = np.ascontiguousarray(np.asarray(L0_ffn_qweight_gu, dtype=np.uint32))
    qzg = np.ascontiguousarray(np.asarray(L0_ffn_qzeros_gu, dtype=np.uint32))
    scg = np.ascontiguousarray(np.asarray(L0_ffn_scales_gu, dtype=np.float32))
    qwd = np.ascontiguousarray(np.asarray(L0_ffn_qweight_down, dtype=np.uint32))
    qzd = np.ascontiguousarray(np.asarray(L0_ffn_qzeros_down, dtype=np.uint32))
    scd = np.ascontiguousarray(np.asarray(L0_ffn_scales_down, dtype=np.float32))
    b8 = qwg.view(np.uint8).reshape(4, 128, 2, 86, 128, 4)
    qprep = np.ascontiguousarray(
        b8.transpose(3, 2, 1, 0, 5, 4)).reshape(86, 2, 128, 2048)
    gidx = 8 * np.arange(4)[None, :] + (np.arange(128) // 16)[:, None]
    sc3 = scg.reshape(32, 2, 86, 128)
    srprep = np.ascontiguousarray(sc3[gidx].transpose(3, 2, 0, 1, 4))
    qdp = np.zeros((11, 128, 4, 4096), np.uint8)
    qdp.reshape(1408, 4, 4096)[:1376] = (
        qwd.view(np.uint8).reshape(1376, 4096, 4).transpose(0, 2, 1))
    srd = np.ascontiguousarray(np.repeat(scd, 16, axis=0))

    if _NC_CACHE[0] is None:
        _NC_CACHE[0] = build()
    nc = _NC_CACHE[0]
    ims = [dict(x=x, x_tok=np.ascontiguousarray(x[c]), qprep=qprep,
                srprep=srprep, sc_gu=scg, qz_gu=qzg, qdprep=qdp, srep_d=srd,
                sc_d=scd, qz_d=qzd) for c in range(8)]
    res = run_bass_kernel_spmd(nc, ims, core_ids=list(range(8)))
    return np.stack([res.results[c]["out"] for c in range(8)], axis=0)


if __name__ == "__main__":
    import time
    t0 = time.time()
    nc = build()
    print("built", nc.n_instructions(), "instructions in",
          round(time.time() - t0, 1), "s")


# version 08
# speedup vs baseline: 1.0885x; 1.0847x over previous
import math
from contextlib import ExitStack

import numpy as np
import concourse.bass as bass
import concourse.mybir as mybir
from concourse.bass_utils import run_bass_kernel_spmd

F32 = mybir.dt.float32
BF16 = mybir.dt.bfloat16
U32 = mybir.dt.uint32
U8 = mybir.dt.uint8
AF = mybir.ActivationFunctionType
ALU = mybir.AluOpType
AX = mybir.AxisListType

B8 = 8
T = 2048
D = 4096
J = 22016
FF = 11008
NTT = 16
NTC = 4
NKT = 32
NA = 86
NCH = 32
HL = 88 * 128
HTT = 128 * HL
EPS = 1e-5
INVS = 1.0 / math.sqrt(2048.0)


class Ctr:
    def __init__(self, ctx, nc, name, step=1):
        self.sem = ctx.enter_context(nc.semaphore(name))
        self.step = step
        self.v = 0

    def inc(self, inst):
        self.v += self.step
        inst.then_inc(self.sem, self.step)
        return self.v


def build():
    nc = bass.Bass("TRN2", target_bir_lowering=False)

    x = nc.dram_tensor("x", [B8, T, D], F32, kind="ExternalInput")
    x_tok = nc.dram_tensor("x_tok", [T, D], F32, kind="ExternalInput")
    qprep = nc.dram_tensor("qprep", [NA, 2, 128, 2048], U8,
                           kind="ExternalInput")
    srprep = nc.dram_tensor("srprep", [NA, 2, 128, 4, 128], F32,
                            kind="ExternalInput")
    sc_gu = nc.dram_tensor("sc_gu", [32, J], F32, kind="ExternalInput")
    qz_gu = nc.dram_tensor("qz_gu", [32, J // 8], U32, kind="ExternalInput")
    qdprep = nc.dram_tensor("qdprep", [11, 128, 4, D], U8,
                            kind="ExternalInput")
    srep_d = nc.dram_tensor("srep_d", [1376, D], F32, kind="ExternalInput")
    sc_d = nc.dram_tensor("sc_d", [86, D], F32, kind="ExternalInput")
    qz_d = nc.dram_tensor("qz_d", [86, D // 8], U32, kind="ExternalInput")
    out = nc.dram_tensor("out", [T, D], F32, kind="ExternalOutput")

    h_dram = nc.dram_tensor("h_dram", [16, 128, 88, 128], BF16,
                            kind="Internal")
    msg_dram = nc.dram_tensor("msg_dram", [T, D], BF16, kind="Internal")
    bgu_dram = nc.dram_tensor("bgu_dram", [32, J], BF16, kind="Internal")
    bd_dram = nc.dram_tensor("bd_dram", [86, D], BF16, kind="Internal")

    xap = x.ap()
    xtk = x_tok.ap()
    out_ap = out.ap()
    bgu_ap = bgu_dram.ap()
    bd_ap = bd_dram.ap()
    msg_ap = msg_dram.ap()

    with ExitStack() as top:
        SQ = Ctr(top, nc, "SQ", 16)
        AQ = Ctr(top, nc, "AQ", 16)
        V = Ctr(top, nc, "V")
        S = Ctr(top, nc, "S")
        TE = Ctr(top, nc, "TE")
        G = Ctr(top, nc, "G")
        P0L = [Ctr(top, nc, "P0L%d" % i, 16) for i in range(4)]
        P2L = [Ctr(top, nc, "P2L%d" % i, 16) for i in range(2)]
        P3L = [Ctr(top, nc, "P3L%d" % i, 16) for i in range(2)]
        SP3 = [Ctr(top, nc, "SP3%d" % i, 16) for i in range(2)]
        P4W = [Ctr(top, nc, "P4W%d" % i, 16) for i in range(2)]
        BDC = Ctr(top, nc, "BDC", 16)
        P4H = [Ctr(top, nc, "P4H%d" % i, 16) for i in range(2)]
        ST4 = [Ctr(top, nc, "ST4%d" % i, 16) for i in range(2)]
        SCD = Ctr(top, nc, "SCD", 16)
        allq = ([SQ, AQ, V, S, TE, G] + P0L + P2L + P3L + SP3 + P4W +
                [BDC] + P4H + ST4 + [SCD])

        sb = lambda *a: top.enter_context(nc.sbuf_tensor(*a))
        ident = sb("ident", [128, 128], BF16)
        scratch_f = sb("scratch_f", [128, 172], F32)
        indneg = sb("indneg", [128, 172], BF16)
        ones1 = sb("ones1", [1, 128], BF16)
        stats_b = sb("stats_b", [128, NTT, 3], F32)
        stats_all = sb("stats_all", [128, B8, NTT, 3], F32)
        nrs_t = sb("nrs_t", [32, T], BF16)
        nrsh_t = sb("nrsh_t", [86, T], BF16)
        S_all = sb("S_all", [128, NCH, 8], F32)
        E_all = sb("E_all", [128, NCH, 8], BF16)

        block = top.enter_context(nc.Block())

        def barrier():
            snap = [(q.sem, q.v) for q in allq if q.v]
            for opener in (block.sync, block.vector, block.scalar,
                           block.tensor, block.gpsimd):
                def f(e, snap=snap):
                    for sem, v in snap:
                        e.wait_ge(sem, v)
                opener(f)

        def init_g(g):
            G.inc(g.iota(scratch_f[:, 0:128], [[-1, 128]], base=0,
                         channel_multiplier=1,
                         allow_small_or_imprecise_dtypes=True))
        block.gpsimd(init_g)
        g1 = G.v

        def init_v1(v):
            v.wait_ge(G.sem, g1)
            V.inc(v.tensor_scalar(out=ident[:], in0=scratch_f[:, 0:128],
                                  scalar1=0.0, scalar2=None, op0=ALU.is_equal))
            V.inc(v.memset(ones1[:], 1.0))
        block.vector(init_v1)
        v1 = V.v

        def init_g2(g):
            g.wait_ge(V.sem, v1)
            G.inc(g.iota(scratch_f[:], [[1, 172]], base=-86,
                         channel_multiplier=0,
                         allow_small_or_imprecise_dtypes=True))
        block.gpsimd(init_g2)
        g2 = G.v

        def init_v2(v):
            v.wait_ge(G.sem, g2)
            V.inc(v.tensor_scalar(out=indneg[:], in0=scratch_f[:], scalar1=0.0,
                                  scalar2=-1.0, op0=ALU.is_equal, op1=ALU.mult))
        block.vector(init_v2)

        with nc.sbuf_tensor("qzb", [32, J // 8], U32) as qzb, \
             nc.sbuf_tensor("zqt", [32, J // 4], U32) as zqt, \
             nc.sbuf_tensor("scb", [32, J // 4], F32) as scb, \
             nc.sbuf_tensor("bgu_x", [32, J // 4], BF16) as bgu_x, \
             nc.sbuf_tensor("qzdb", [86, D // 8], U32) as qzdb, \
             nc.sbuf_tensor("zqdt", [86, D], U32) as zqdt, \
             nc.sbuf_tensor("scdb", [86, D], F32) as scdb, \
             nc.sbuf_tensor("bd_x", [86, D], BF16) as bd_x:
            CJ = J // 4
            CP = J // 32

            def p1a_l0(sy):
                SQ.inc(sy.dma_start(out=qzb[:], in_=qz_gu.ap()))
                SQ.inc(sy.dma_start(out=qzdb[:], in_=qz_d.ap()))
            block.sync(p1a_l0)
            ld0 = SQ.v

            spill_prev = [0]
            vtok_prev = [0]
            for cc in range(4):
                def p1a_ls(sy, cc=cc):
                    if vtok_prev[0]:
                        sy.wait_ge(V.sem, vtok_prev[0])
                    SQ.inc(sy.dma_start(
                        out=scb[:], in_=sc_gu.ap()[:, cc * CJ:(cc + 1) * CJ]))
                block.sync(p1a_ls)
                scld = SQ.v

                def p1a_v(v, cc=cc, scld=scld):
                    v.wait_ge(SQ.sem, max(ld0, scld))
                    if spill_prev[0]:
                        v.wait_ge(AQ.sem, spill_prev[0])
                    for k in range(8):
                        V.inc(v.tensor_scalar(
                            out=bass.AP(zqt, k, [[CJ, 32], [8, CP]]),
                            in0=qzb[:, cc * CP:(cc + 1) * CP],
                            scalar1=4 * k, scalar2=15,
                            op0=ALU.logical_shift_right, op1=ALU.bitwise_and))
                    v.wait_ge(V.sem, V.v)
                    V.inc(v.tensor_scalar(out=zqt[:], in0=zqt[:], scalar1=1,
                                          scalar2=None, op0=ALU.add))
                    v.wait_ge(V.sem, V.v)
                    V.inc(v.tensor_tensor(out=bgu_x[:], in0=zqt[:], in1=scb[:],
                                          op=ALU.mult))
                block.vector(p1a_v)
                vtok_prev[0] = V.v
                vtok = V.v

                def p1a_s(sc, cc=cc, vtok=vtok):
                    sc.wait_ge(V.sem, vtok)
                    AQ.inc(sc.dma_start(out=bgu_ap[:, cc * CJ:(cc + 1) * CJ],
                                        in_=bgu_x[:]))
                block.scalar(p1a_s)
                spill_prev[0] = AQ.v

            def p1a_ld(sy):
                SCD.inc(sy.dma_start(out=scdb[:], in_=sc_d.ap()))
            block.sync(p1a_ld)
            scdld = SCD.v

            def p1a_vd(v, scdld=scdld):
                v.wait_ge(SCD.sem, scdld)
                for k in range(8):
                    V.inc(v.tensor_scalar(
                        out=bass.AP(zqdt, k, [[D, 86], [8, D // 8]]),
                        in0=qzdb[:], scalar1=4 * k, scalar2=15,
                        op0=ALU.logical_shift_right, op1=ALU.bitwise_and))
                v.wait_ge(V.sem, V.v)
                V.inc(v.tensor_scalar(out=zqdt[:], in0=zqdt[:], scalar1=1,
                                      scalar2=None, op0=ALU.add))
                v.wait_ge(V.sem, V.v)
                V.inc(v.tensor_tensor(out=bd_x[:], in0=zqdt[:], in1=scdb[:],
                                      op=ALU.mult))
            block.vector(p1a_vd)
            vdt = V.v

            def p1a_sd(sc, vdt=vdt):
                sc.wait_ge(V.sem, vdt)
                AQ.inc(sc.dma_start(out=bd_ap, in_=bd_x[:]))
            block.scalar(p1a_sd)

            barrier()

        with nc.sbuf_tensor("xnT", [128, NKT, T], BF16) as xnT:

            with nc.sbuf_tensor("xst", [128, D], F32) as xst, \
                 nc.sbuf_tensor("p1scrap", [128, D // 2], F32) as p1scrap, \
                 nc.sbuf_tensor("xnbf", [128, D], BF16) as xnbf, \
                 nc.sbuf_tensor("xnpm", [128, D], BF16) as xnpm, \
                 nc.sbuf_tensor("p1tmp", [128, 8], F32) as p1tmp, \
                 nc.sbuf_tensor("nrspf", [128, 32], F32) as nrspf, \
                 nc.sbuf_tensor("nrsp", [128, 32], BF16) as nrsp, \
                 nc.psum_tensor("pstr0", [128, 128], BF16) as pstr0, \
                 nc.psum_tensor("pstr1", [128, 128], BF16) as pstr1, \
                 nc.psum_tensor("psnrs", [32, 128], BF16) as psnrs:
                pstr = [pstr0, pstr1]
                prev = {"sxn": 0, "nrse": 0, "vvar": 0}
                for tt in range(NTT):
                    def p1_l(sy, tt=tt):
                        if prev["sxn"]:
                            sy.wait_ge(S.sem, prev["sxn"])
                        if prev["nrse"]:
                            sy.wait_ge(V.sem, prev["nrse"])
                        SQ.inc(sy.dma_start(
                            out=xst[:], in_=xtk[tt * 128:(tt + 1) * 128, :]))
                    block.sync(p1_l)
                    xld = SQ.v

                    def p1_v1(v, xld=xld):
                        v.wait_ge(SQ.sem, xld)
                        V.inc(v.reduce_sum(out=p1tmp[:, 0:1], in_=xst[:, :D // 2],
                                           axis=AX.X))
                        V.inc(v.reduce_sum(out=p1tmp[:, 1:2], in_=xst[:, D // 2:],
                                           axis=AX.X))
                    block.vector(p1_v1)
                    vsum = V.v

                    def p1_s1(sc, xld=xld):
                        sc.wait_ge(SQ.sem, xld)
                        if prev["vvar"]:
                            sc.wait_ge(V.sem, prev["vvar"])
                        S.inc(sc.activation(out=p1scrap[:], in_=xst[:, :D // 2],
                                            func=AF.Square,
                                            accum_out=p1tmp[:, 2:3]))
                        sc.wait_ge(S.sem, S.v)
                        S.inc(sc.activation(out=p1scrap[:], in_=xst[:, D // 2:],
                                            func=AF.Square,
                                            accum_out=p1tmp[:, 3:4]))
                    block.scalar(p1_s1)
                    ssq = S.v

                    def p1_v2(v, tt=tt, vsum=vsum, ssq=ssq):
                        v.wait_ge(V.sem, vsum)
                        v.wait_ge(S.sem, ssq)
                        V.inc(v.tensor_scalar(out=stats_b[:, tt, 0:1],
                                              in0=p1tmp[:, 0:1],
                                              scalar1=p1tmp[:, 1:2],
                                              scalar2=1.0 / D,
                                              op0=ALU.add, op1=ALU.mult))
                        V.inc(v.tensor_scalar(out=p1tmp[:, 5:6],
                                              in0=p1tmp[:, 2:3],
                                              scalar1=p1tmp[:, 3:4],
                                              scalar2=1.0 / D,
                                              op0=ALU.add, op1=ALU.mult))
                        v.wait_ge(V.sem, V.v)
                        V.inc(v.tensor_tensor(out=p1tmp[:, 6:7],
                                              in0=stats_b[:, tt, 0:1],
                                              in1=stats_b[:, tt, 0:1],
                                              op=ALU.mult))
                        v.wait_ge(V.sem, V.v)
                        V.inc(v.tensor_scalar(out=p1tmp[:, 5:6],
                                              in0=p1tmp[:, 5:6],
                                              scalar1=p1tmp[:, 6:7], scalar2=EPS,
                                              op0=ALU.subtract, op1=ALU.add))
                    block.vector(p1_v2)
                    vvar = V.v
                    prev["vvar"] = vvar

                    def p1_s2(sc, vvar=vvar):
                        sc.wait_ge(V.sem, vvar)
                        S.inc(sc.activation(out=p1tmp[:, 7:8], in_=p1tmp[:, 5:6],
                                            func=AF.Sqrt))
                    block.scalar(p1_s2)
                    ssd = S.v

                    def p1_v3(v, tt=tt, ssd=ssd):
                        v.wait_ge(S.sem, ssd)
                        V.inc(v.reciprocal(out=stats_b[:, tt, 1:2],
                                           in_=p1tmp[:, 7:8]))
                        v.wait_ge(V.sem, V.v)
                        V.inc(v.tensor_scalar(out=stats_b[:, tt, 2:3],
                                              in0=stats_b[:, tt, 0:1],
                                              scalar1=stats_b[:, tt, 1:2],
                                              scalar2=-1.0, op0=ALU.mult,
                                              op1=ALU.mult))
                    block.vector(p1_v3)
                    vst = V.v

                    def p1_s3(sc, tt=tt, vst=vst):
                        sc.wait_ge(V.sem, vst)
                        if prev["nrse"]:
                            sc.wait_ge(V.sem, prev["nrse"])
                        S.inc(sc.activation(out=xnbf[:], in_=xst[:],
                                            func=AF.Identity,
                                            scale=stats_b[:, tt, 1:2],
                                            bias=stats_b[:, tt, 2:3]))
                    block.scalar(p1_s3)
                    sxn = S.v
                    prev["sxn"] = sxn

                    def p1_v4(v, tt=tt, sxn=sxn):
                        v.wait_ge(S.sem, sxn)
                        if prev["nrse"]:
                            v.wait_ge(V.sem, prev["nrse"])
                        V.inc(v.tensor_copy(
                            out=xnpm[:],
                            in_=bass.AP(xnbf, 0,
                                        [[D, 128], [1024, 4], [1, 8], [8, 128]])))
                        V.inc(v.tensor_reduce(
                            out=nrspf[:],
                            in_=bass.AP(xnbf, 0, [[D, 128], [128, 32], [1, 128]]),
                            axis=AX.X, op=ALU.add, negate=True))
                        v.wait_ge(V.sem, V.v)
                        V.inc(v.tensor_copy(out=nrsp[:], in_=nrspf[:]))
                    block.vector(p1_v4)
                    vperm = V.v

                    evac_base = V.v

                    def p1_t(te, tt=tt, vperm=vperm, evac_base=evac_base):
                        te.wait_ge(V.sem, vperm)
                        for kt in range(NKT):
                            if kt >= 2:
                                te.wait_ge(V.sem, evac_base + kt - 1)
                            TE.inc(te.transpose(pstr[kt % 2][:],
                                                xnpm[:, kt * 128:(kt + 1) * 128],
                                                ident[:]))
                        TE.inc(te.transpose(psnrs[:], nrsp[:], ident[:]))
                    tr_base = TE.v
                    block.tensor(p1_t)

                    def p1_v5(v, tt=tt, tr_base=tr_base):
                        for kt in range(NKT):
                            v.wait_ge(TE.sem, tr_base + kt + 1)
                            V.inc(v.tensor_copy(
                                out=xnT[:, kt, tt * 128:(tt + 1) * 128],
                                in_=pstr[kt % 2][:]))
                        v.wait_ge(TE.sem, tr_base + NKT + 1)
                        V.inc(v.tensor_copy(out=nrs_t[:, tt * 128:(tt + 1) * 128],
                                            in_=psnrs[:]))
                    block.vector(p1_v5)
                    assert V.v == evac_base + NKT + 1
                    prev["nrse"] = V.v

            barrier()

            with nc.sbuf_tensor("xst0", [128, 4, D // 2], F32) as xst0, \
                 nc.sbuf_tensor("p0scrap", [128, D // 2], F32) as p0scrap, \
                 nc.sbuf_tensor("p0sum", [128, NTT, 2], F32) as p0sum, \
                 nc.sbuf_tensor("p0sq", [128, NTT, 2], F32) as p0sq, \
                 nc.sbuf_tensor("p0t", [128, NTT, 4], F32) as p0t:
                prev_v3 = [0]
                for m in range(B8):
                    vb = V.v
                    sbase = S.v
                    p0toks = []

                    def p0_l(sy, m=m, vb=vb, sbase=sbase, p0toks=p0toks):
                        for i in range(32):
                            if i >= 4:
                                sy.wait_ge(V.sem, vb + i - 3)
                                sy.wait_ge(S.sem, sbase + i - 3)
                            elif m > 0:
                                sy.wait_ge(V.sem, vb)
                                sy.wait_ge(S.sem, sbase)
                            tt, hh = divmod(i, 2)
                            p0toks.append(P0L[i % 4].inc(sy.dma_start(
                                out=xst0[:, i % 4, :],
                                in_=xap[m, tt * 128:(tt + 1) * 128,
                                        hh * (D // 2):(hh + 1) * (D // 2)])))
                    block.sync(p0_l)

                    def p0_v1(v, m=m, p0toks=p0toks):
                        if prev_v3[0]:
                            v.wait_ge(V.sem, prev_v3[0])
                        for i in range(32):
                            tt, hh = divmod(i, 2)
                            v.wait_ge(P0L[i % 4].sem, p0toks[i])
                            V.inc(v.reduce_sum(out=p0sum[:, tt, hh:hh + 1],
                                               in_=xst0[:, i % 4, :], axis=AX.X))
                    block.vector(p0_v1)
                    assert V.v == vb + 32

                    def p0_s1(sc, m=m, p0toks=p0toks):
                        if prev_v3[0]:
                            sc.wait_ge(V.sem, prev_v3[0])
                        for i in range(32):
                            tt, hh = divmod(i, 2)
                            sc.wait_ge(P0L[i % 4].sem, p0toks[i])
                            S.inc(sc.activation(out=p0scrap[:],
                                                in_=xst0[:, i % 4, :],
                                                func=AF.Square,
                                                accum_out=p0sq[:, tt, hh:hh + 1]))
                    block.scalar(p0_s1)
                    assert S.v == sbase + 32
                    s1 = S.v
                    v1b = V.v

                    def p0_v2(v, m=m, s1=s1, v1b=v1b):
                        v.wait_ge(V.sem, v1b)
                        v.wait_ge(S.sem, s1)
                        for tt in range(NTT):
                            V.inc(v.tensor_scalar(
                                out=stats_all[:, m, tt, 0:1],
                                in0=p0sum[:, tt, 0:1], scalar1=p0sum[:, tt, 1:2],
                                scalar2=1.0 / D, op0=ALU.add, op1=ALU.mult))
                            V.inc(v.tensor_scalar(
                                out=p0t[:, tt, 0:1],
                                in0=p0sq[:, tt, 0:1], scalar1=p0sq[:, tt, 1:2],
                                scalar2=1.0 / D, op0=ALU.add, op1=ALU.mult))
                        v.wait_ge(V.sem, V.v)
                        for tt in range(NTT):
                            V.inc(v.tensor_tensor(out=p0t[:, tt, 1:2],
                                                  in0=stats_all[:, m, tt, 0:1],
                                                  in1=stats_all[:, m, tt, 0:1],
                                                  op=ALU.mult))
                        v.wait_ge(V.sem, V.v)
                        for tt in range(NTT):
                            V.inc(v.tensor_scalar(out=p0t[:, tt, 2:3],
                                                  in0=p0t[:, tt, 0:1],
                                                  scalar1=p0t[:, tt, 1:2],
                                                  scalar2=EPS, op0=ALU.subtract,
                                                  op1=ALU.add))
                    block.vector(p0_v2)
                    v2 = V.v

                    def p0_s2(sc, m=m, v2=v2):
                        sc.wait_ge(V.sem, v2)
                        for tt in range(NTT):
                            S.inc(sc.activation(out=p0t[:, tt, 3:4],
                                                in_=p0t[:, tt, 2:3],
                                                func=AF.Sqrt))
                    block.scalar(p0_s2)
                    s2 = S.v

                    def p0_v3(v, m=m, s2=s2):
                        v.wait_ge(S.sem, s2)
                        for tt in range(NTT):
                            V.inc(v.reciprocal(out=stats_all[:, m, tt, 1:2],
                                               in_=p0t[:, tt, 3:4]))
                        v.wait_ge(V.sem, V.v)
                        for tt in range(NTT):
                            V.inc(v.tensor_scalar(out=stats_all[:, m, tt, 2:3],
                                                  in0=stats_all[:, m, tt, 0:1],
                                                  scalar1=stats_all[:, m, tt, 1:2],
                                                  scalar2=-1.0, op0=ALU.mult,
                                                  op1=ALU.mult))
                    block.vector(p0_v3)
                    prev_v3[0] = V.v

            barrier()

            with nc.sbuf_tensor("xq", [128, B8, NTT, 128], BF16) as xq, \
                 nc.sbuf_tensor("xqb", [128, NTT, 128], BF16) as xqb, \
                 nc.sbuf_tensor("xst2", [128, 2, NTT, 128], F32) as xst2, \
                 nc.sbuf_tensor("macc", [128, NTT, 128], BF16) as macc, \
                 nc.sbuf_tensor("mprod", [128, NTT, 128], BF16) as mprod, \
                 nc.sbuf_tensor("msk", [128, 8, 128], BF16) as msk, \
                 nc.sbuf_tensor("p2sc", [128, 128], F32) as p2sc, \
                 nc.sbuf_tensor("et_sb", [1, 1024], BF16) as et_sb, \
                 nc.sbuf_tensor("erep", [128, 1024], BF16) as erep, \
                 nc.sbuf_tensor("p2t", [128, 4], F32) as p2t, \
                 nc.sbuf_tensor("e_f", [128, 8], F32) as e_f, \
                 nc.psum_tensor("gps0", [128, 128], F32) as gps0, \
                 nc.psum_tensor("gps1", [128, 128], F32) as gps1, \
                 nc.psum_tensor("etps", [1, 1024], BF16) as etps, \
                 nc.psum_tensor("erps0", [128, 512], F32) as erps0, \
                 nc.psum_tensor("erps1", [128, 512], F32) as erps1:
                gps = [gps0, gps1]
                erps = [erps0, erps1]
                pc = {"gram_last": 0, "diag_last": 0, "spill": 0,
                      "msg_last": 0, "ete": 0, "erepe": 0, "norm_last": 0,
                      "exp": 0, "sacc": 0}
                for ch in range(NCH):
                    norm_toks = []
                    for m in range(B8 + 1):
                        def p2_l(sy, ch=ch, m=m):
                            if m >= 2:
                                sy.wait_ge(V.sem, norm_toks[m - 2])
                            elif pc["norm_last"]:
                                sy.wait_ge(V.sem, pc["norm_last"])
                            if m < B8:
                                src = xap[m].rearrange("(ct p) l -> p ct l",
                                                       p=128)[:, :,
                                                              ch * 128:(ch + 1) * 128]
                                P2L[m % 2].inc(sy.dma_start(out=xst2[:, m % 2],
                                                            in_=src))
                            else:
                                src = xtk.rearrange("(ct p) l -> p ct l",
                                                    p=128)[:, :,
                                                           ch * 128:(ch + 1) * 128]
                                P2L[m % 2].inc(sy.dma_start(out=xst2[:, m % 2],
                                                            in_=src))
                        block.sync(p2_l)
                        lt = P2L[m % 2].v

                        def p2_n(v, ch=ch, m=m, lt=lt):
                            v.wait_ge(P2L[m % 2].sem, lt)
                            if m == 0 and pc["gram_last"]:
                                v.wait_ge(TE.sem, pc["gram_last"])
                            st = stats_all[:, m] if m < B8 else stats_b
                            for ct in range(NTT):
                                dst = xq[:, m, ct, :] if m < B8 else xqb[:, ct, :]
                                V.inc(v.tensor_scalar(
                                    out=dst, in0=xst2[:, m % 2, ct, :],
                                    scalar1=st[:, ct, 1:2],
                                    scalar2=st[:, ct, 2:3],
                                    op0=ALU.mult, op1=ALU.add))
                        block.vector(p2_n)
                        norm_toks.append(V.v)
                    pc["norm_last"] = V.v

                    diag_pred = V.v

                    def p2_g(te, ch=ch, diag_pred=diag_pred):
                        te.wait_ge(V.sem, pc["norm_last"])
                        for m in range(B8):
                            if m >= 2:
                                te.wait_ge(V.sem, diag_pred + (m - 1))
                            elif pc["diag_last"]:
                                te.wait_ge(V.sem, pc["diag_last"])
                            for ct in range(NTT):
                                mm = te.matmul(gps[m % 2][:], xqb[:, ct, :],
                                               xq[:, m, ct, :],
                                               start=(ct == 0),
                                               stop=(ct == NTT - 1))
                                if ct == NTT - 1:
                                    TE.inc(mm)
                    gram_base = TE.v
                    block.tensor(p2_g)
                    pc["gram_last"] = TE.v

                    def p2_d(v, ch=ch, gram_base=gram_base):
                        if pc["sacc"]:
                            v.wait_ge(S.sem, pc["sacc"])
                        for m in range(B8):
                            v.wait_ge(TE.sem, gram_base + m + 1)
                            V.inc(v.tensor_tensor(out=msk[:, m],
                                                  in0=gps[m % 2][:],
                                                  in1=ident[:], op=ALU.mult))
                    block.vector(p2_d)
                    assert V.v == diag_pred + B8

                    def p2_sd(sc, ch=ch, vd0=diag_pred):
                        for m in range(B8):
                            sc.wait_ge(V.sem, vd0 + m + 1)
                            S.inc(sc.activation(out=p2sc[:], in_=msk[:, m],
                                                func=AF.Identity,
                                                accum_out=S_all[:, ch,
                                                                m:m + 1]))
                    block.scalar(p2_sd)
                    pc["sacc"] = S.v
                    sacc = S.v

                    def p2_d2(v, ch=ch, sacc=sacc):
                        v.wait_ge(S.sem, sacc)
                        if pc["exp"]:
                            v.wait_ge(S.sem, pc["exp"])
                        V.inc(v.reduce_max(out=p2t[:, 0:1], in_=S_all[:, ch, :],
                                           axis=AX.X))
                        v.wait_ge(V.sem, V.v)
                        V.inc(v.tensor_scalar(out=p2t[:, 1:2], in0=p2t[:, 0:1],
                                              scalar1=-INVS, scalar2=None,
                                              op0=ALU.mult))
                    block.vector(p2_d2)
                    pc["diag_last"] = V.v
                    vdiag = V.v

                    def p2_e(sc, ch=ch, vdiag=vdiag, sacc=sacc):
                        sc.wait_ge(V.sem, vdiag)
                        sc.wait_ge(S.sem, sacc)
                        S.inc(sc.activation(out=e_f[:], in_=S_all[:, ch, :],
                                            func=AF.Exp, scale=INVS,
                                            bias=p2t[:, 1:2]))
                    block.scalar(p2_e)
                    pc["exp"] = S.v
                    sexp = S.v

                    def p2_sm(v, ch=ch, sexp=sexp):
                        v.wait_ge(S.sem, sexp)
                        V.inc(v.reduce_sum(out=p2t[:, 2:3], in_=e_f[:],
                                           axis=AX.X))
                        v.wait_ge(V.sem, V.v)
                        V.inc(v.reciprocal(out=p2t[:, 3:4], in_=p2t[:, 2:3]))
                        v.wait_ge(V.sem, V.v)
                        V.inc(v.tensor_scalar(out=E_all[:, ch, :], in0=e_f[:],
                                              scalar1=p2t[:, 3:4], scalar2=None,
                                              op0=ALU.mult))
                    block.vector(p2_sm)
                    ven = V.v

                    def p2_t1(te, ch=ch, ven=ven):
                        te.wait_ge(V.sem, ven)
                        if pc["ete"]:
                            te.wait_ge(V.sem, pc["ete"])
                        for m in range(B8):
                            TE.inc(te.transpose(etps[0:1, m * 128:(m + 1) * 128],
                                                E_all[:, ch, m:m + 1], ident[:]))
                    block.tensor(p2_t1)
                    ttr = TE.v

                    def p2_ev(v, ch=ch, ttr=ttr):
                        v.wait_ge(TE.sem, ttr)
                        V.inc(v.tensor_copy(out=et_sb[:], in_=etps[:]))
                    block.vector(p2_ev)
                    pc["ete"] = V.v
                    vet = V.v

                    def p2_t2(te, ch=ch, vet=vet):
                        te.wait_ge(V.sem, vet)
                        if pc["erepe"]:
                            te.wait_ge(V.sem, pc["erepe"])
                        for m in range(B8):
                            TE.inc(te.matmul(
                                erps[m // 4][:, (m % 4) * 128:(m % 4 + 1) * 128],
                                ones1[:], et_sb[0:1, m * 128:(m + 1) * 128],
                                start=True, stop=True))
                    block.tensor(p2_t2)
                    terep = TE.v

                    def p2_m(v, ch=ch, terep=terep):
                        v.wait_ge(TE.sem, terep)
                        if pc["spill"]:
                            v.wait_ge(AQ.sem, pc["spill"])
                        V.inc(v.tensor_copy(out=erep[:, 0:512], in_=erps[0][:]))
                        V.inc(v.tensor_copy(out=erep[:, 512:1024],
                                            in_=erps[1][:]))
                        v.wait_ge(V.sem, V.v)
                        pc["erepe"] = V.v
                        for m in range(B8):
                            bcast = bass.AP(erep, m * 128,
                                            [[1024, 128], [0, NTT], [1, 128]])
                            if m == 0:
                                V.inc(v.tensor_tensor(out=macc[:], in0=xq[:, 0],
                                                      in1=bcast, op=ALU.mult))
                            else:
                                V.inc(v.tensor_tensor(out=mprod[:],
                                                      in0=xq[:, m], in1=bcast,
                                                      op=ALU.mult))
                                v.wait_ge(V.sem, V.v)
                                V.inc(v.tensor_tensor(out=macc[:], in0=macc[:],
                                                      in1=mprod[:], op=ALU.add))
                            v.wait_ge(V.sem, V.v)
                    block.vector(p2_m)
                    pc["msg_last"] = V.v
                    vmsg = V.v

                    def p2_sp(sc, ch=ch, vmsg=vmsg):
                        sc.wait_ge(V.sem, vmsg)
                        dst = bass.AP(msg_dram, ch * 128,
                                      [[D, 128], [128 * D, NTT], [1, 128]])
                        AQ.inc(sc.dma_start(out=dst, in_=macc[:]))
                    block.scalar(p2_sp)
                    pc["spill"] = AQ.v

            barrier()

            with nc.sbuf_tensor("wring", [128, 2, 2, 2, 4, 4, 128], BF16) as wring, \
                 nc.sbuf_tensor("pkr", [128, 2, 2, 2048], U8) as pkr, \
                 nc.sbuf_tensor("srr", [128, 2, 2, 4, 128], F32) as srr, \
                 nc.sbuf_tensor("bgur", [32, 2, 2, 128], BF16) as bgur, \
                 nc.sbuf_tensor("nl3", [128, 2048], U8) as nl3, \
                 nc.sbuf_tensor("nh3", [128, 2048], U8) as nh3, \
                 nc.sbuf_tensor("hstage", [128, 8, 512], BF16) as hstage, \
                 nc.sbuf_tensor("sig", [128, 2, 512], F32) as sig, \
                 nc.psum_tensor("pg0", [128, 512], F32) as pg0, \
                 nc.psum_tensor("pg1", [128, 512], F32) as pg1, \
                 nc.psum_tensor("pu0", [128, 512], F32) as pu0, \
                 nc.psum_tensor("pu1", [128, 512], F32) as pu1, \
                 nc.psum_tensor("rs0", [86, 512], F32) as rs0, \
                 nc.psum_tensor("rs1", [86, 512], F32) as rs1, \
                 nc.psum_tensor("rs2", [86, 512], F32) as rs2, \
                 nc.psum_tensor("rs3", [86, 512], F32) as rs3:
                pg = [pg0, pg1]
                pu = [pu0, pu1]
                rsp = [rs0, rs1, rs2, rs3]

                def p3_loads(sy, a, war_tok, te_tok):
                    if war_tok:
                        sy.wait_ge(V.sem, war_tok)
                    if te_tok:
                        sy.wait_ge(TE.sem, te_tok)
                    sl = a % 2
                    qp = qprep.ap()
                    sp = srprep.ap()
                    for s, base in ((0, 0), (1, FF)):
                        off = base + a * 128
                        P3L[sl].inc(sy.dma_start(out=pkr[:, sl, s],
                                                 in_=qp[a, s]))
                        P3L[sl].inc(sy.dma_start(out=srr[:, sl, s],
                                                 in_=sp[a, s]))
                        P3L[sl].inc(sy.dma_start(out=bgur[:, sl, s],
                                                 in_=bgu_ap[:, off:off + 128]))

                block.sync(lambda sy: p3_loads(sy, 0, 0, 0))
                ld_tok = {0: P3L[0].v}
                deq_tok = {}
                hT_tok = {}
                silu_tok = {}
                spill_tok3 = {}
                mm_tok = {}

                for a in range(NA):
                    def p3_l(sy, a=a):
                        if a + 1 < NA:
                            p3_loads(sy, a + 1, deq_tok.get(a - 1, 0),
                                     mm_tok.get(a - 1, 0))
                    block.sync(p3_l)
                    if a + 1 < NA:
                        ld_tok[a + 1] = P3L[(a + 1) % 2].v

                    def p3_dq(v, a=a):
                        v.wait_ge(P3L[a % 2].sem, ld_tok[a])
                        if a >= 2:
                            v.wait_ge(TE.sem, mm_tok[a - 2])
                        sl = a % 2
                        for s in range(2):
                            V.inc(v.tensor_scalar(
                                out=nl3[:], in0=pkr[:, sl, s, :],
                                scalar1=15, scalar2=None, op0=ALU.bitwise_and))
                            V.inc(v.tensor_scalar(
                                out=nh3[:], in0=pkr[:, sl, s, :],
                                scalar1=4, scalar2=None,
                                op0=ALU.logical_shift_right))
                            v.wait_ge(V.sem, V.v)
                            in1 = bass.AP(srr, (sl * 2 + s) * 512,
                                          [[2048, 128], [128, 4],
                                           [0, 4], [1, 128]])
                            V.inc(v.tensor_tensor(
                                out=wring[:, sl, s, 0], in0=nl3[:],
                                in1=in1, op=ALU.mult))
                            V.inc(v.tensor_tensor(
                                out=wring[:, sl, s, 1], in0=nh3[:],
                                in1=in1, op=ALU.mult))
                            v.wait_ge(V.sem, V.v)
                    block.vector(p3_dq)
                    deq_tok[a] = V.v
                    vpred = V.v
                    silu_pred = S.v
                    mmb = TE.v

                    def p3_mm(te, a=a, vpred=vpred, silu_pred=silu_pred):
                        te.wait_ge(V.sem, deq_tok[a])
                        sl = a % 2
                        for tc in range(NTC):
                            if tc >= 2:
                                te.wait_ge(S.sem, silu_pred + tc - 1)
                                te.wait_ge(V.sem, vpred + tc - 1)
                            elif a >= 1:
                                te.wait_ge(S.sem, silu_tok[a - 1])
                                te.wait_ge(V.sem, hT_tok[a - 1])
                            for s, bank in ((0, pg[tc % 2]), (1, pu[tc % 2])):
                                for kt in range(NKT):
                                    wsl = wring[:, sl, s, kt % 2, kt // 8,
                                                (kt % 8) // 2, :]
                                    te.matmul(bank[:], wsl,
                                              xnT[:, kt, tc * 512:(tc + 1) * 512],
                                              start=(kt == 0), stop=False)
                                TE.inc(te.matmul(
                                    bank[:], bgur[:, sl, s],
                                    nrs_t[:, tc * 512:(tc + 1) * 512],
                                    start=False, stop=True))
                        if a >= 1:
                            for tc in range(NTC):
                                slot = ((a - 1) * NTC + tc) % 8
                                TE.inc(te.matmul(
                                    rsp[tc][:],
                                    indneg[:, 86 - (a - 1):172 - (a - 1)],
                                    hstage[:, slot, :],
                                    start=(a - 1 == 0), stop=False))
                    block.tensor(p3_mm)
                    mm_tok[a] = TE.v

                    def p3_si(sc, a=a, mmb=mmb, vpred=vpred):
                        for tc in range(NTC):
                            sc.wait_ge(TE.sem, mmb + 2 * tc + 1)
                            if tc >= 2:
                                sc.wait_ge(V.sem, vpred + tc - 1)
                            elif a >= 1:
                                sc.wait_ge(V.sem, hT_tok[a - 1])
                            S.inc(sc.activation(out=sig[:, tc % 2, :],
                                                in_=pg[tc % 2][:],
                                                func=AF.Silu))
                    block.scalar(p3_si)
                    assert S.v == silu_pred + NTC
                    silu_tok[a] = S.v

                    def p3_h(v, a=a, mmb=mmb, silu_pred=silu_pred):
                        for tc in range(NTC):
                            slot = (a * NTC + tc) % 8
                            v.wait_ge(S.sem, silu_pred + tc + 1)
                            v.wait_ge(TE.sem, mmb + 2 * tc + 2)
                            if spill_tok3.get(a - 2) is not None:
                                v.wait_ge(SP3[a % 2].sem, spill_tok3[a - 2])
                            V.inc(v.tensor_tensor(out=hstage[:, slot, :],
                                                  in0=sig[:, tc % 2, :],
                                                  in1=pu[tc % 2][:],
                                                  op=ALU.mult))
                    block.vector(p3_h)
                    assert V.v == vpred + NTC
                    hT_tok[a] = V.v
                    hbase = V.v - NTC

                    def p3_sp(sc, a=a, hbase=hbase):
                        for tc in range(NTC):
                            slot = (a * NTC + tc) % 8
                            sc.wait_ge(V.sem, hbase + tc + 1)
                            if a < 80:
                                off = (16 * (a % 8)) * HL + (8 * (a // 8)) * 128
                            else:
                                off = (16 * (a - 80)) * HL + 80 * 128
                            for c in range(4):
                                dst = bass.AP(h_dram,
                                              (tc * 4 + c) * HTT + off,
                                              [[HL, 16], [128, 8], [1, 128]])
                                SP3[a % 2].inc(sc.dma_start(
                                    out=dst,
                                    in_=hstage[:, slot,
                                               c * 128:(c + 1) * 128]))
                        spill_tok3[a] = SP3[a % 2].v
                    block.gpsimd(p3_sp)

                def p3_ep_t(te):
                    a = NA - 1
                    te.wait_ge(V.sem, hT_tok[a])
                    for tc in range(NTC):
                        slot = (a * NTC + tc) % 8
                        TE.inc(te.matmul(rsp[tc][:], indneg[:, 86 - a:172 - a],
                                         hstage[:, slot, :], start=False,
                                         stop=True))
                block.tensor(p3_ep_t)
                rs_done = TE.v

                def p3_ep_v(v, rs_done=rs_done):
                    v.wait_ge(TE.sem, rs_done)
                    for tc in range(NTC):
                        V.inc(v.tensor_copy(
                            out=nrsh_t[:, tc * 512:(tc + 1) * 512],
                            in_=rsp[tc][:]))
                block.vector(p3_ep_v)

                barrier()

        with nc.sbuf_tensor("wd", [128, 2, 11, 4, 512], BF16) as wd, \
             nc.sbuf_tensor("pk4", [128, 2, 2048], U8) as pk4, \
             nc.sbuf_tensor("sr4", [128, 2, 512], F32) as sr4, \
             nc.sbuf_tensor("nl4", [128, 2048], U8) as nl4, \
             nc.sbuf_tensor("nh4", [128, 2048], U8) as nh4, \
             nc.sbuf_tensor("bd4", [86, 512], BF16) as bd4, \
             nc.sbuf_tensor("hbuf", [128, 2, 88, 128], BF16) as hbuf, \
             nc.sbuf_tensor("xpc", [128, 2, 512], F32) as xpc, \
             nc.sbuf_tensor("xnpc", [128, 2, 512], F32) as xnpc, \
             nc.sbuf_tensor("msgpc", [128, 2, 512], BF16) as msgpc, \
             nc.sbuf_tensor("osb", [128, 2, 512], F32) as osb, \
             nc.psum_tensor("od0", [128, 512], F32) as od0, \
             nc.psum_tensor("od1", [128, 512], F32) as od1:
            od = [od0, od1]

            def p4_hload(sy, d, tt):
                par = tt % 2
                P4H[par].inc(sy.dma_start(
                    out=hbuf[:, par, :, :],
                    in_=bass.AP(h_dram, tt * HTT,
                                [[HL, 128], [128, 88], [1, 128]])))
                P4H[par].inc(sy.dma_start(
                    out=xpc[:, par, :],
                    in_=xtk[tt * 128:(tt + 1) * 128, d * 512:(d + 1) * 512]))
                P4H[par].inc(sy.dma_start(
                    out=msgpc[:, par, :],
                    in_=msg_ap[tt * 128:(tt + 1) * 128,
                               d * 512:(d + 1) * 512]))
                return P4H[par].v

            mm4_tok = {}
            add_tok = {}
            st_tok = {}
            dq4_last = [0]
            for d in range(8):
                vb4 = V.v
                wtok = {}
                ld4 = {}

                def p4_l(sy, d=d, vb4=vb4, wtok=wtok, ld4=ld4):
                    if d > 0:
                        sy.wait_ge(TE.sem, mm4_tok[(d - 1, NTT - 1)])
                        sy.wait_ge(V.sem, add_tok[(d - 1, NTT - 1)])
                    BDC.inc(sy.dma_start(out=bd4[:],
                                         in_=bd_ap[:, d * 512:(d + 1) * 512]))
                    for blk in range(11):
                        if blk >= 2:
                            sy.wait_ge(V.sem, vb4 + 4 * (blk - 1))
                        kp = 128 if blk < 10 else 96
                        P4W[blk % 2].inc(sy.dma_start(
                            out=pk4[0:kp, blk % 2, :],
                            in_=qdprep.ap()[blk, 0:kp, :,
                                            d * 512:(d + 1) * 512]))
                        wtok[blk] = P4W[blk % 2].inc(sy.dma_start(
                            out=sr4[0:kp, blk % 2, :],
                            in_=bass.AP(srep_d, (128 * blk) * D + d * 512,
                                        [[D, kp], [1, 512]])))
                    ld4[0] = p4_hload(sy, d, 0)
                block.sync(p4_l)
                bdtok = BDC.v

                def p4_dq(v, d=d, wtok=wtok):
                    if d > 0:
                        v.wait_ge(TE.sem, mm4_tok[(d - 1, NTT - 1)])
                    for blk in range(11):
                        kp = 128 if blk < 10 else 96
                        v.wait_ge(P4W[blk % 2].sem, wtok[blk])
                        V.inc(v.tensor_scalar(
                            out=nl4[0:kp, :], in0=pk4[0:kp, blk % 2, :],
                            scalar1=15, scalar2=None, op0=ALU.bitwise_and))
                        V.inc(v.tensor_scalar(
                            out=nh4[0:kp, :], in0=pk4[0:kp, blk % 2, :],
                            scalar1=4, scalar2=None,
                            op0=ALU.logical_shift_right))
                        v.wait_ge(V.sem, V.v)
                        in1 = bass.AP(sr4, (blk % 2) * 512,
                                      [[1024, kp], [0, 4], [1, 512]])
                        V.inc(v.tensor_tensor(out=wd[0:kp, 0, blk],
                                              in0=nl4[0:kp, :], in1=in1,
                                              op=ALU.mult))
                        V.inc(v.tensor_tensor(out=wd[0:kp, 1, blk],
                                              in0=nh4[0:kp, :], in1=in1,
                                              op=ALU.mult))
                        v.wait_ge(V.sem, V.v)
                block.vector(p4_dq)
                assert V.v == vb4 + 4 * 11
                dq4_last[0] = V.v

                for tt in range(NTT):
                    def p4_lt(sy, d=d, tt=tt, ld4=ld4):
                        if tt + 1 < NTT:
                            if tt >= 1:
                                sy.wait_ge(TE.sem, mm4_tok[(d, tt - 1)])
                                sy.wait_ge(V.sem, add_tok[(d, tt - 1)])
                            ld4[tt + 1] = p4_hload(sy, d, tt + 1)
                    block.sync(p4_lt)

                    def p4_mm(te, d=d, tt=tt, bdtok=bdtok, vb4=vb4):
                        te.wait_ge(P4H[tt % 2].sem, ld4[tt])
                        if tt == 0:
                            te.wait_ge(BDC.sem, bdtok)
                        else:
                            te.wait_ge(V.sem, dq4_last[0])
                        if add_tok.get((d, tt - 2)) is not None:
                            te.wait_ge(V.sem, add_tok[(d, tt - 2)])
                        elif d > 0 and tt < 2:
                            te.wait_ge(V.sem, add_tok[(d - 1, NTT - 2 + tt)])
                        par = tt % 2
                        for kt in range(88):
                            kp = 128 if kt < 80 else 96
                            if tt == 0 and kt % 8 == 0:
                                te.wait_ge(V.sem, vb4 + 4 * (kt // 8 + 1))
                            te.matmul(od[par][:], hbuf[0:kp, par, kt, :],
                                      wd[0:kp, kt % 2, kt // 8,
                                         (kt % 8) // 2, :],
                                      start=(kt == 0), stop=False)
                        TE.inc(te.matmul(od[par][:],
                                         nrsh_t[:, tt * 128:(tt + 1) * 128],
                                         bd4[:], start=False, stop=True))
                    block.tensor(p4_mm)
                    mm4_tok[(d, tt)] = TE.v

                    def p4_xn(sc, d=d, tt=tt):
                        sc.wait_ge(P4H[tt % 2].sem, ld4[tt])
                        if add_tok.get((d, tt - 2)) is not None:
                            sc.wait_ge(V.sem, add_tok[(d, tt - 2)])
                        elif d > 0 and tt < 2:
                            sc.wait_ge(V.sem, add_tok[(d - 1, NTT - 2 + tt)])
                        S.inc(sc.activation(out=xnpc[:, tt % 2, :],
                                            in_=xpc[:, tt % 2, :],
                                            func=AF.Identity,
                                            scale=stats_b[:, tt, 1:2],
                                            bias=stats_b[:, tt, 2:3]))
                    block.scalar(p4_xn)
                    xnt4 = S.v

                    def p4_add(v, d=d, tt=tt, xnt4=xnt4):
                        par = tt % 2
                        v.wait_ge(TE.sem, mm4_tok[(d, tt)])
                        v.wait_ge(S.sem, xnt4)
                        if st_tok.get((d, tt - 2)) is not None:
                            v.wait_ge(ST4[tt % 2].sem, st_tok[(d, tt - 2)])
                        elif d > 0 and tt < 2:
                            v.wait_ge(ST4[tt % 2].sem,
                                      st_tok[(d - 1, NTT - 2 + tt)])
                        V.inc(v.tensor_tensor(out=osb[:, par, :],
                                              in0=od[par][:],
                                              in1=xpc[:, par, :], op=ALU.add))
                        v.wait_ge(V.sem, V.v)
                        V.inc(v.tensor_tensor(out=osb[:, par, :],
                                              in0=osb[:, par, :],
                                              in1=xnpc[:, par, :], op=ALU.add))
                        v.wait_ge(V.sem, V.v)
                        V.inc(v.tensor_tensor(out=osb[:, par, :],
                                              in0=osb[:, par, :],
                                              in1=msgpc[:, par, :],
                                              op=ALU.add))
                    block.vector(p4_add)
                    add_tok[(d, tt)] = V.v

                    def p4_st(sc, d=d, tt=tt):
                        sc.wait_ge(V.sem, add_tok[(d, tt)])
                        ST4[tt % 2].inc(sc.dma_start(
                            out=out_ap[tt * 128:(tt + 1) * 128,
                                       d * 512:(d + 1) * 512],
                            in_=osb[:, tt % 2, :]))
                    block.scalar(p4_st)
                    st_tok[(d, tt)] = ST4[tt % 2].v

            fins = [(ST4[0].sem, ST4[0].v), (ST4[1].sem, ST4[1].v),
                    (AQ.sem, AQ.v)]

            def p4_fin(sy, fins=fins):
                for sem, vv in fins:
                    sy.wait_ge(sem, vv)
            block.sync(p4_fin)

    return nc


_NC_CACHE = [None]


def kernel(L0_attn_x, L0_ffn_qweight_gu, L0_ffn_qzeros_gu, L0_ffn_scales_gu,
           L0_ffn_g_idx_gu, L0_ffn_qweight_down, L0_ffn_qzeros_down,
           L0_ffn_scales_down, L0_ffn_g_idx_down):
    x = np.ascontiguousarray(np.asarray(L0_attn_x, dtype=np.float32))
    qwg = np.ascontiguousarray(np.asarray(L0_ffn_qweight_gu, dtype=np.uint32))
    qzg = np.ascontiguousarray(np.asarray(L0_ffn_qzeros_gu, dtype=np.uint32))
    scg = np.ascontiguousarray(np.asarray(L0_ffn_scales_gu, dtype=np.float32))
    qwd = np.ascontiguousarray(np.asarray(L0_ffn_qweight_down, dtype=np.uint32))
    qzd = np.ascontiguousarray(np.asarray(L0_ffn_qzeros_down, dtype=np.uint32))
    scd = np.ascontiguousarray(np.asarray(L0_ffn_scales_down, dtype=np.float32))
    b8 = qwg.view(np.uint8).reshape(4, 128, 2, 86, 128, 4)
    qprep = np.ascontiguousarray(
        b8.transpose(3, 2, 1, 0, 5, 4)).reshape(86, 2, 128, 2048)
    gidx = 8 * np.arange(4)[None, :] + (np.arange(128) // 16)[:, None]
    sc3 = scg.reshape(32, 2, 86, 128)
    srprep = np.ascontiguousarray(sc3[gidx].transpose(3, 2, 0, 1, 4))
    qdp = np.zeros((11, 128, 4, 4096), np.uint8)
    qdp.reshape(1408, 4, 4096)[:1376] = (
        qwd.view(np.uint8).reshape(1376, 4096, 4).transpose(0, 2, 1))
    srd = np.ascontiguousarray(np.repeat(scd, 16, axis=0))

    if _NC_CACHE[0] is None:
        _NC_CACHE[0] = build()
    nc = _NC_CACHE[0]
    ims = [dict(x=x, x_tok=np.ascontiguousarray(x[c]), qprep=qprep,
                srprep=srprep, sc_gu=scg, qz_gu=qzg, qdprep=qdp, srep_d=srd,
                sc_d=scd, qz_d=qzd) for c in range(8)]
    res = run_bass_kernel_spmd(nc, ims, core_ids=list(range(8)))
    return np.stack([res.results[c]["out"] for c in range(8)], axis=0)


if __name__ == "__main__":
    import time
    t0 = time.time()
    nc = build()
    print("built", nc.n_instructions(), "instructions in",
          round(time.time() - t0, 1), "s")
